# version 12
# baseline (speedup 1.0000x reference)
import numpy as np
import concourse.bass as bass
import concourse.mybir as mybir
from concourse.ap import AP
from concourse.bass_utils import run_bass_kernel_spmd

F32 = mybir.dt.float32
BF16 = mybir.dt.bfloat16
I8 = mybir.dt.int8
AF = mybir.ActivationFunctionType
ALU = mybir.AluOpType

D = 2048
S = 2048
NCH = 16
NH = 16
NKV = 8
INTER = 8192
NIC = 64
EPS = 1e-5
ISQ = float(1.0 / np.sqrt(128.0))
T = 512
NT = S // T
TA = 512
NTA = S // TA


def bc_ap(ap2d, reps):
    return AP(tensor=ap2d.tensor, offset=ap2d.offset,
              ap=[list(ap2d.ap[0]), [0, reps], list(ap2d.ap[1])])


class Sem:
    __slots__ = ("h", "n")

    def __init__(self, stack, nc, name):
        self.h = stack.enter_context(nc.semaphore(name))
        self.n = 0


def build_program():
    from contextlib import ExitStack
    nc = bass.Bass("TRN2", target_bir_lowering=False,
                   detect_race_conditions=False)
    waits = []

    xR = nc.dram_tensor("xR", [128, NT, NCH, T], F32, kind="ExternalInput").ap()
    wq_r = nc.dram_tensor("wq_r", [128, NH * D], F32, kind="ExternalInput").ap()
    wk_r = nc.dram_tensor("wk_r", [128, NCH * 1024], F32, kind="ExternalInput").ap()
    wv_r = nc.dram_tensor("wv_r", [128, NCH * 1024], F32, kind="ExternalInput").ap()
    wo_r = nc.dram_tensor("wo_r", [128, NCH * D], F32, kind="ExternalInput").ap()
    wgu_r = nc.dram_tensor("wgu_r", [128, NIC // 2, 2, 2 * D], I8,
                           kind="ExternalInput").ap()
    wd_n = nc.dram_tensor("wd_n", [INTER, D], I8, kind="ExternalInput").ap()
    sg_r = nc.dram_tensor("sg_r", [128, NIC], F32, kind="ExternalInput").ap()
    su_r = nc.dram_tensor("su_r", [128, NIC], F32, kind="ExternalInput").ap()
    sd_r = nc.dram_tensor("sd_r", [128, NCH], F32, kind="ExternalInput").ap()
    import os
    _dbg = os.environ.get("K_DBG_SCRATCH") == "1"
    outR = nc.dram_tensor("outR", [128, NT, NCH, T], F32, kind="ExternalOutput").ap()
    attno = nc.dram_tensor("attno", [128, NT, NCH, T], BF16,
                           kind="ExternalOutput" if _dbg else "Internal").ap()
    xmid = nc.dram_tensor("xmid", [128, NT, NCH, T], F32,
                          kind="ExternalOutput" if _dbg else "Internal").ap()
    lx3 = nc.dram_tensor("lx3", [128, NT, NCH, T], BF16, kind="Internal").ap()


    stack = nc.ctx

    s_sc = Sem(stack, nc, "s_sc")
    s_x = [Sem(stack, nc, "s_x0"), Sem(stack, nc, "s_x1")]
    s_xb = Sem(stack, nc, "s_xb")
    s_xc = Sem(stack, nc, "s_xc")
    st_a = [Sem(stack, nc, "st_a0"), Sem(stack, nc, "st_a1")]
    st_x = Sem(stack, nc, "st_x")
    st_o = Sem(stack, nc, "st_o")
    st_lx = Sem(stack, nc, "st_lx")
    s_lx = Sem(stack, nc, "s_lx")
    g_wk = Sem(stack, nc, "g_wk")
    g_wv = Sem(stack, nc, "g_wv")
    g_wo = Sem(stack, nc, "g_wo")
    g_q = [Sem(stack, nc, "g_q0"), Sem(stack, nc, "g_q1")]
    g_g = [Sem(stack, nc, "g_g0"), Sem(stack, nc, "g_g1")]
    g_u = [Sem(stack, nc, "g_u0"), Sem(stack, nc, "g_u1")]
    g_d = [Sem(stack, nc, "g_d0"), Sem(stack, nc, "g_d1")]
    t_c = Sem(stack, nc, "t_c")
    v_c = Sem(stack, nc, "v_c")
    c_c = Sem(stack, nc, "c_c")
    t_b = Sem(stack, nc, "t_b")
    v_b = Sem(stack, nc, "v_b")
    c_b = Sem(stack, nc, "c_b")
    COMPUTE_SEMS = (t_c, v_c, c_c, t_b, v_b, c_b)
    ENG_SEMS = {"tensor": (t_c, t_b), "vector": (v_c, v_b), "scalar": (c_c, c_b)}

    def _inc(inst, sem, k=1):
        inst.then_inc(sem.h, k)
        sem.n += k
        return sem.n

    def _wait(eng, sem, v):
        if v <= 0:
            return
        waits.append((sem, v))
        eng.wait_ge(sem.h, v)

    def SEC(blk, name):
        def deco(body):
            dec = getattr(blk, name)

            @dec
            def _(eng):
                pr = ENG_SEMS.get(name)
                if pr:
                    _wait(eng, pr[0], pr[0].n)
                    _wait(eng, pr[1], pr[1].n)
                last = body(eng)
                if pr and last is not None:
                    _inc(last, pr[0], 1)
            return None
        return deco

    def block_guard(blk):
        for name in ("tensor", "vector", "scalar"):
            dec = getattr(blk, name)
            snap = [(s, s.n) for s in COMPUTE_SEMS]

            @dec
            def _(eng, snap=snap):
                for s, v in snap:
                    _wait(eng, s, v)

    ones_bf = stack.enter_context(nc.sbuf_tensor("ones_bf", [128, 128], BF16))
    ones_f = stack.enter_context(nc.sbuf_tensor("ones_f", [128, 128], F32))
    sg_sb = stack.enter_context(nc.sbuf_tensor("sg_sb", [128, NIC], F32))
    su_sb = stack.enter_context(nc.sbuf_tensor("su_sb", [128, NIC], F32))
    sd_sb = stack.enter_context(nc.sbuf_tensor("sd_sb", [128, NCH], F32))
    lnm = stack.enter_context(nc.sbuf_tensor("lnm", [128, 512], F32))
    lnv = stack.enter_context(nc.sbuf_tensor("lnv", [128, 512], F32))
    lnt = stack.enter_context(nc.sbuf_tensor("lnt", [128, 512], F32))
    ab_bf = stack.enter_context(nc.sbuf_tensor("ab_bf", [128, 1024], BF16))


    res_scope = ExitStack()
    ln1x = res_scope.enter_context(nc.sbuf_tensor("ln1x", [128, NCH, S], BF16))
    K_sb = res_scope.enter_context(nc.sbuf_tensor("K_sb", [128, NKV, S], BF16))
    V_sb = res_scope.enter_context(nc.sbuf_tensor("V_sb", [128, NCH, 1024], BF16))

    a_scope = ExitStack()
    wk_bf = a_scope.enter_context(nc.sbuf_tensor("a2_wk", [128, NCH, 1024], BF16))

    with nc.Block() as blk:
        @SEC(blk, "vector")
        def _(v):
            v.memset(ones_bf[:], 1.0)
            return v.memset(ones_f[:], 1.0)

        @blk.gpsimd
        def _(g):
            _inc(g.dma_start(out=wk_bf[:],
                             in_=wk_r.rearrange("p (c k) -> p c k", c=NCH)),
                 g_wk, 16)

        @blk.sync
        def _(s):
            _inc(s.dma_start(out=sg_sb[:], in_=sg_r[:]), s_sc, 16)
            _wait(s, s_sc, 16)
            _inc(s.dma_start(out=su_sb[:], in_=su_r[:]), s_sc, 16)
            _wait(s, s_sc, 32)
            _inc(s.dma_start(out=sd_sb[:], in_=sd_r[:]), s_sc, 16)
    scales_ld = s_sc.n
    wk_rdy = g_wk.n

    def emit_ln(blk, ps, in_bf, sq_out, Tt, pre_sq=None):
        @SEC(blk, "vector")
        def _(v):
            if pre_sq is not None:
                pre_sq(v)
            return v.tensor_tensor(sq_out, in_bf, in_bf, ALU.mult)
        v_sq = v_c.n

        @SEC(blk, "tensor")
        def _(t):
            _wait(t, v_c, v_sq)
            for c in range(NCH):
                t.matmul(ps[:, 2048:2048 + Tt], ones_bf[:, 0:128],
                         in_bf[:, c, :], start=(c == 0), stop=(c == NCH - 1),
                         skip_group_check=True)
            i = None
            for c in range(NCH):
                i = t.matmul(ps[:, 2560:2560 + Tt], ones_bf[:, 0:128],
                             sq_out[:, c, :], start=(c == 0), stop=(c == NCH - 1),
                             skip_group_check=True)
            return i
        t_sums = t_c.n

        @SEC(blk, "vector")
        def _(v):
            _wait(v, t_c, t_sums)
            v.tensor_scalar_mul(lnm[:, 0:Tt], ps[:, 2048:2048 + Tt], 1.0 / D)
            v.tensor_scalar_mul(lnv[:, 0:Tt], ps[:, 2560:2560 + Tt], 1.0 / D)
            v.tensor_tensor(lnt[:, 0:Tt], lnm[:, 0:Tt], lnm[:, 0:Tt], ALU.mult)
            v.tensor_tensor(lnv[:, 0:Tt], lnv[:, 0:Tt], lnt[:, 0:Tt], ALU.subtract)
            v.tensor_scalar_add(lnv[:, 0:Tt], lnv[:, 0:Tt], EPS)
            return v.reciprocal(lnv[:, 0:Tt], lnv[:, 0:Tt])
        v_rv = v_c.n

        @SEC(blk, "scalar")
        def _(sc):
            _wait(sc, v_c, v_rv)
            return sc.activation(lnv[:, 0:Tt], lnv[:, 0:Tt], AF.Sqrt)
        c_sqrt = c_c.n

        @SEC(blk, "vector")
        def _(v):
            _wait(v, c_c, c_sqrt)
            v.tensor_copy(ab_bf[:, 0:Tt], lnv[:, 0:Tt])
            return v.scalar_tensor_tensor(ab_bf[:, 512:512 + Tt], lnm[:, 0:Tt],
                                          -1.0, lnv[:, 0:Tt], ALU.mult, ALU.mult)
        return v_c.n

    with (
        nc.sbuf_tensor("a1_xbf", [128, NCH, TA], BF16) as xbf,
        nc.psum_tensor("a1_ps", [128, 4096], F32) as ps,
    ):
        with nc.Block() as blk:
            block_guard(blk)
            xa_ld = [None] * NTA
            v_nm = [None] * NTA

            @blk.gpsimd
            def _(g):
                _inc(g.dma_start(out=xbf[:], in_=xR[:, 0]), g_g[0], 16)
            xa_ld[0] = g_g[0].n

            for ti in range(NTA):
                t0 = ti * TA
                lslice = ln1x[:, :, t0:t0 + TA]

                def _wx(v, ti=ti):
                    _wait(v, g_g[0], xa_ld[ti])
                emit_ln(blk, ps, xbf[:], lslice, TA, pre_sq=_wx)

                @SEC(blk, "vector")
                def _(v, ti=ti, lslice=lslice):
                    v.tensor_tensor(lslice, xbf[:], bc_ap(ab_bf[:, 0:TA], NCH),
                                    ALU.mult)
                    return v.tensor_tensor(lslice, lslice,
                                           bc_ap(ab_bf[:, 512:512 + TA], NCH),
                                           ALU.add)
                v_nm[ti] = v_c.n

                if ti + 1 < NTA:
                    @blk.gpsimd
                    def _(g, ti=ti):
                        _wait(g, v_c, v_nm[ti])
                        _inc(g.dma_start(out=xbf[:], in_=xR[:, ti + 1]),
                             g_g[0], 16)
                    xa_ld[ti + 1] = g_g[0].n

    with (
        nc.sbuf_tensor("a2_wv", [128, NCH, 1024], BF16) as wv_bf,
        nc.psum_tensor("a2_ps", [128, 4096], F32) as ps,
    ):
        with nc.Block() as blk:
            block_guard(blk)

            @blk.gpsimd
            def _(g):
                _inc(g.dma_start(out=wv_bf[:],
                                 in_=wv_r.rearrange("p (c k) -> p c k", c=NCH)),
                     g_wv, 16)
            wv_rdy = g_wv.n

            ev = [None] * (NKV * NT)
            for ckv in range(NKV):
                for tt in range(NT):
                    it = ckv * NT + tt

                    @SEC(blk, "tensor")
                    def _(t, ckv=ckv, tt=tt, it=it):
                        if it == 0:
                            _wait(t, g_wk, wk_rdy)
                        if it >= 2:
                            _wait(t, v_c, ev[it - 2])
                        pcol = (it % 2) * 512
                        i = None
                        for c in range(NCH):
                            i = t.matmul(ps[:, pcol:pcol + 512],
                                         wk_bf[:, c, ckv * 128:(ckv + 1) * 128],
                                         ln1x[:, c, tt * T:(tt + 1) * T],
                                         start=(c == 0), stop=(c == NCH - 1),
                                         skip_group_check=True)
                        return i
                    tk = t_c.n

                    @SEC(blk, "vector")
                    def _(v, ckv=ckv, tt=tt, it=it, tk=tk):
                        _wait(v, t_c, tk)
                        pcol = (it % 2) * 512
                        return v.tensor_copy(K_sb[:, ckv, tt * T:(tt + 1) * T],
                                             ps[:, pcol:pcol + 512])
                    ev[it] = v_c.n

            evv = [None] * NCH
            for ck in range(NCH):
                @SEC(blk, "tensor")
                def _(t, ck=ck):
                    if ck == 0:
                        _wait(t, g_wv, wv_rdy)
                        _wait(t, v_c, ev[NKV * NT - 1])
                    if ck >= 2:
                        _wait(t, c_c, evv[ck - 2])
                    pcol = (ck % 2) * 1024
                    i = None
                    for c in range(NCH):
                        for nh in range(2):
                            i = t.matmul(
                                ps[:, pcol + nh * 512:pcol + nh * 512 + 512],
                                ln1x[:, c, ck * 128:(ck + 1) * 128],
                                wv_bf[:, c, nh * 512:(nh + 1) * 512],
                                start=(c == 0), stop=(c == NCH - 1),
                                skip_group_check=True)
                    return i
                tv = t_c.n

                @SEC(blk, "scalar")
                def _(sc, ck=ck, tv=tv):
                    _wait(sc, t_c, tv)
                    pcol = (ck % 2) * 1024
                    return sc.copy(V_sb[:, ck, :], ps[:, pcol:pcol + 1024])
                evv[ck] = c_c.n

    a_scope.close()

    with (
        nc.sbuf_tensor("b1_wq", [128, 2, D], BF16) as wq_bf,
        nc.sbuf_tensor("b1_qbf", [128, 2, T], BF16) as qbf,
        nc.sbuf_tensor("b1_probs", [128, 2, NCH * T], BF16) as probs,
        nc.sbuf_tensor("b1_obf", [128, 2, T], BF16) as obf,
        nc.sbuf_tensor("b1_osc", [128, 2, T], BF16) as osc,
        nc.sbuf_tensor("b1_rec", [128, T], F32) as rec,
        nc.sbuf_tensor("b1_tree", [128, 2, 8, T], BF16) as tree,
        nc.psum_tensor("b1_ps", [128, 4096], F32) as ps,
    ):
        PO, PQS = 3072, 3584
        with nc.Block() as blk:
            block_guard(blk)

            n_it = NH * NT
            wq_ld = [None] * NH
            tq_done = [None] * (n_it + 1)
            vq_done = [None] * (n_it + 1)
            sc_exp = [None] * (n_it * 8)
            tpv_done = [None] * n_it
            sums_done = [None] * n_it
            tree_done = [None] * n_it
            rc_done = [None] * n_it
            obf_done = [None] * n_it
            osc_done = [None] * n_it
            st_ao = [None] * n_it

            @blk.gpsimd
            def _(g):
                _inc(g.dma_start(out=wq_bf[:, 0], in_=wq_r[:, 0:D]), g_q[0], 16)
            wq_ld[0] = g_q[0].n

            def kv_of(j):
                return (j // NT) // 2

            def emit_q(t, j):
                h, tt = j // NT, j % NT
                if tt == 0:
                    _wait(t, g_q[h % 2], wq_ld[h])
                if j >= 3:
                    _wait(t, v_b, rc_done[j - 3])
                _wait(t, v_c, vq_done[j - 1] if j >= 1 else 0)
                i = None
                for c in range(NCH):
                    i = t.matmul(ps[:, PQS:PQS + 512],
                                 wq_bf[:, h % 2, c * 128:(c + 1) * 128],
                                 ln1x[:, c, tt * T:(tt + 1) * T],
                                 start=(c == 0), stop=(c == NCH - 1),
                                 skip_group_check=True)
                return i

            @SEC(blk, "tensor")
            def _(t):
                return emit_q(t, 0)
            tq_done[0] = t_c.n

            @SEC(blk, "vector")
            def _(v):
                _wait(v, t_c, tq_done[0])
                return v.tensor_copy(qbf[:, 0], ps[:, PQS:PQS + 512])
            vq_done[0] = v_c.n

            C_base = c_c.n
            exp_jobs = []
            for g2 in range(n_it * 8):
                sc_exp[g2] = C_base + g2 + 1

            for i in range(n_it):
                h, tt = i // NT, i % NT
                kvh = h // 2

                if tt == 0 and h + 1 < NH:
                    @blk.gpsimd
                    def _(g, h=h):
                        _wait(g, t_c, tq_done[h * NT - 1] if h >= 1 else 0)
                        _inc(g.dma_start(out=wq_bf[:, (h + 1) % 2],
                                         in_=wq_r[:, (h + 1) * D:(h + 2) * D]),
                             g_q[(h + 1) % 2], 16)
                    wq_ld[h + 1] = g_q[(h + 1) % 2].n

                c0 = C_base + 8 * i

                @SEC(blk, "tensor")
                def _(t, i=i, kvh=kvh, c0=c0):
                    _wait(t, v_c, vq_done[i])
                    if i >= 1:
                        _wait(t, c_c, sc_exp[(i - 1) * 8 + 7])
                        if i >= 2:
                            _wait(t, v_b, obf_done[i - 2])
                    m_pv = None
                    for g in range(8):
                        _wait(t, c_c, c0 + g - 2)
                        base = (g % 3) * 1024
                        for j in range(2):
                            kc = 2 * g + j
                            m = t.matmul(
                                ps[:, base + j * 512:base + j * 512 + 512],
                                K_sb[:, kvh, kc * 128:(kc + 1) * 128],
                                qbf[:, i % 2],
                                start=True, stop=True, skip_group_check=True)
                        _inc(m, t_c, 1)
                        if i >= 1:
                            for j in range(2):
                                kc = 2 * g + j
                                m_pv = t.matmul(
                                    ps[:, PO:PO + 512],
                                    V_sb[:, kc, kv_of(i - 1) * 128:
                                         kv_of(i - 1) * 128 + 128],
                                    probs[:, (i - 1) % 2, kc * T:(kc + 1) * T],
                                    start=(kc == 0), stop=(kc == NCH - 1),
                                    skip_group_check=True)
                    if i >= 1:
                        tpv_done[i - 1] = _inc(m_pv, t_b, 1)
                t_sc_base = t_c.n - 8

                if i + 1 < n_it:
                    @SEC(blk, "tensor")
                    def _(t, i=i):
                        return emit_q(t, i + 1)
                    tq_done[i + 1] = t_c.n

                exp_jobs.append((i, t_sc_base))

                if i + 1 < n_it:
                    @SEC(blk, "vector")
                    def _(v, i=i):
                        _wait(v, t_c, tq_done[i + 1])
                        return v.tensor_copy(qbf[:, (i + 1) % 2],
                                             ps[:, PQS:PQS + 512])
                    vq_done[i + 1] = v_c.n

                @SEC(blk, "tensor")
                def _(t, i=i):
                    if i >= 1:
                        _wait(t, v_c, vq_done[min(i + 1, n_it - 1)])
                        _wait(t, v_c, tree_done[i - 1])
                        m = t.matmul(ps[:, PQS:PQS + 512], ones_bf[:, 0:128],
                                     tree[:, (i - 1) % 2, 0, :],
                                     start=True, stop=True, skip_group_check=True)
                        sums_done[i - 1] = _inc(m, t_b, 1)
                    return None

                @SEC(blk, "vector")
                def _(v, i=i):
                    _wait(v, c_c, sc_exp[i * 8 + 7])
                    if i >= 2:
                        _wait(v, t_b, sums_done[i - 2])
                    tr = tree[:, i % 2]
                    v.tensor_tensor(tr[:, :, :],
                                    probs[:, i % 2, 0:8 * T].rearrange(
                                        "p (a t) -> p a t", a=8),
                                    probs[:, i % 2, 8 * T:16 * T].rearrange(
                                        "p (a t) -> p a t", a=8), ALU.add)
                    v.tensor_tensor(tr[:, 0:4, :], tr[:, 0:4, :], tr[:, 4:8, :],
                                    ALU.add)
                    v.tensor_tensor(tr[:, 0:2, :], tr[:, 0:2, :], tr[:, 2:4, :],
                                    ALU.add)
                    return v.tensor_tensor(tr[:, 0:1, :], tr[:, 0:1, :],
                                           tr[:, 1:2, :], ALU.add)
                tree_done[i] = v_c.n

                @SEC(blk, "vector")
                def _(v, i=i):
                    if i >= 2:
                        if i >= 4:
                            _wait(v, st_a[i % 2], st_ao[i - 4])
                        m = v.tensor_tensor(osc[:, i % 2], obf[:, i % 2],
                                            rec[:], ALU.mult)
                        osc_done[i - 2] = _inc(m, v_b, 1)
                    if i >= 1:
                        _wait(v, t_b, tpv_done[i - 1])
                        m = v.tensor_copy(obf[:, (i - 1) % 2], ps[:, PO:PO + 512])
                        obf_done[i - 1] = _inc(m, v_b, 1)
                        _wait(v, t_b, sums_done[i - 1])
                        m = v.reciprocal(rec[:], ps[:, PQS:PQS + 512])
                        rc_done[i - 1] = _inc(m, v_b, 1)
                    return None

                if i >= 2:
                    @blk.sync
                    def _(s, i=i):
                        j = i - 2
                        hh, ttt = j // NT, j % NT
                        _wait(s, v_b, osc_done[j])
                        _inc(s.dma_start(out=attno[:, ttt, hh, :],
                                         in_=osc[:, j % 2]), st_a[j % 2], 16)
                    st_ao[i - 2] = st_a[i % 2].n

            @SEC(blk, "scalar")
            def _(sc):
                for i2, tsb in exp_jobs:
                    if i2 >= 1:
                        if i2 >= 2:
                            _wait(sc, t_b, tpv_done[i2 - 2])
                    for g in range(8):
                        _wait(sc, t_c, tsb + g + 1)
                        base = (g % 3) * 1024
                        m = sc.activation(
                            probs[:, i2 % 2, 2 * g * T:(2 * g + 2) * T],
                            ps[:, base:base + 1024], AF.Exp, scale=ISQ)
                        got = _inc(m, c_c, 1)
                        assert got == sc_exp[i2 * 8 + g], (i2, g, got)
                return None

            @SEC(blk, "tensor")
            def _(t):
                i = n_it
                _wait(t, c_c, sc_exp[(i - 1) * 8 + 7])
                _wait(t, v_b, obf_done[i - 2])
                m = None
                for kc in range(NCH):
                    m = t.matmul(ps[:, PO:PO + 512],
                                 V_sb[:, kc, kv_of(i - 1) * 128:
                                      kv_of(i - 1) * 128 + 128],
                                 probs[:, (i - 1) % 2, kc * T:(kc + 1) * T],
                                 start=(kc == 0), stop=(kc == NCH - 1),
                                 skip_group_check=True)
                tpv_done[i - 1] = _inc(m, t_b, 1)
                _wait(t, v_b, rc_done[i - 2])
                _wait(t, v_c, tree_done[i - 1])
                m = t.matmul(ps[:, PQS:PQS + 512], ones_bf[:, 0:128],
                             tree[:, (i - 1) % 2, 0, :],
                             start=True, stop=True, skip_group_check=True)
                sums_done[i - 1] = _inc(m, t_b, 1)
                return None

            for i in (n_it, n_it + 1):
                @SEC(blk, "vector")
                def _(v, i=i):
                    _wait(v, st_a[i % 2], st_ao[i - 4])
                    if i - 2 < n_it:
                        m = v.tensor_tensor(osc[:, i % 2], obf[:, (i - 2) % 2],
                                            rec[:], ALU.mult)
                        osc_done[i - 2] = _inc(m, v_b, 1)
                    if i - 1 < n_it:
                        _wait(v, t_b, tpv_done[i - 1])
                        m = v.tensor_copy(obf[:, (i - 1) % 2], ps[:, PO:PO + 512])
                        obf_done[i - 1] = _inc(m, v_b, 1)
                        _wait(v, t_b, sums_done[i - 1])
                        m = v.reciprocal(rec[:], ps[:, PQS:PQS + 512])
                        rc_done[i - 1] = _inc(m, v_b, 1)
                    return None

                @blk.sync
                def _(s, i=i):
                    j = i - 2
                    hh, ttt = j // NT, j % NT
                    _wait(s, v_b, osc_done[j])
                    _inc(s.dma_start(out=attno[:, ttt, hh, :],
                                     in_=osc[:, j % 2]), st_a[j % 2], 16)
                st_ao[i - 2] = st_a[i % 2].n

            b1_store_total = (st_a[0].n, st_a[1].n)

    res_scope.close()

    st_xm = [None] * NT
    lx_st = [None] * NT
    with (
        nc.sbuf_tensor("b2_wo", [128, NCH, D], BF16) as wo_bf,
        nc.sbuf_tensor("b2_ao", [128, 2, NCH, T], BF16) as ao,
        nc.sbuf_tensor("b2_xa", [128, NCH, T], F32) as xa,
        nc.sbuf_tensor("b2_aoo", [128, 2, NCH, T], BF16) as aoo,
        nc.sbuf_tensor("b2_sq", [128, NCH, T], BF16) as sqb,
        nc.psum_tensor("b2_ps", [128, 4096], F32) as ps,
    ):
        with nc.Block() as blk:
            block_guard(blk)

            @blk.gpsimd
            def _(g):
                _wait(g, st_a[0], b1_store_total[0])
                _wait(g, st_a[1], b1_store_total[1])
                _inc(g.dma_start(out=wo_bf[:],
                                 in_=wo_r.rearrange("p (c o) -> p c o", c=NCH)),
                     g_wo, 16)
            wo_rdy = g_wo.n

            ao_ld = []

            @blk.sync
            def _(s):
                _wait(s, st_a[0], b1_store_total[0])
                _wait(s, st_a[1], b1_store_total[1])
                _inc(s.dma_start(out=ao[:, 0], in_=attno[:, 0]), s_x[0], 16)
                ao_ld.append(s_x[0].n)

            ev = [None] * (NT * NCH)
            twm_l = []
            v3_done = [None] * NT
            vlx_done = [None] * NT
            xa_ld = [None] * NT
            for tt in range(NT + 1):
                t2 = tt - 1

                if tt < NT and tt + 1 < NT:
                    @blk.sync
                    def _(s, tt=tt):
                        if tt >= 1:
                            _wait(s, c_c, ev[(tt - 1) * NCH + NCH - 1])
                        _inc(s.dma_start(out=ao[:, (tt + 1) % 2],
                                         in_=attno[:, tt + 1]),
                             s_x[(tt + 1) % 2], 16)
                        ao_ld.append(s_x[(tt + 1) % 2].n)

                if t2 >= 0:
                    @blk.sync
                    def _(s, t2=t2):
                        if t2 >= 1:
                            _wait(s, v_c, vlx_done[t2 - 1])
                            _wait(s, st_x, st_xm[t2 - 1])
                        _inc(s.dma_start(out=xa[:], in_=xR[:, t2]), s_xb, 16)
                    xa_ld[t2] = s_xb.n

                if tt < NT:
                    e_base = c_c.n
                    for m in range(NCH):
                        ev[tt * NCH + m] = e_base + m + 1

                    for m in range(NCH):
                        mi = tt * NCH + m

                        @SEC(blk, "tensor")
                        def _(t, tt=tt, m=m, mi=mi):
                            if m == 0:
                                _wait(t, g_wo, wo_rdy)
                                _wait(t, s_x[tt % 2], ao_ld[tt])
                            if mi >= 2:
                                _wait(t, c_c, ev[mi - 2])
                            pcol = (m % 2) * 512
                            i = None
                            for c in range(NCH):
                                i = t.matmul(ps[:, pcol:pcol + 512],
                                             wo_bf[:, c, m * 128:(m + 1) * 128],
                                             ao[:, tt % 2, c, :],
                                             start=(c == 0), stop=(c == NCH - 1),
                                             skip_group_check=True)
                            return i
                        twm_l.append(t_c.n)

                if tt < NT:
                    @SEC(blk, "scalar")
                    def _(sc, tt=tt, twm_s=list(twm_l[-NCH:])):
                        m2 = None
                        for m in range(NCH):
                            _wait(sc, t_c, twm_s[m])
                            if m == 0 and tt >= 2:
                                _wait(sc, v_c, vlx_done[tt - 2])
                            m2 = sc.copy(aoo[:, tt % 2, m, :],
                                         ps[:, (m % 2) * 512:(m % 2) * 512 + 512])
                            got = _inc(m2, c_c, 1)
                            assert got == ev[tt * NCH + m], (tt, m, got)
                        return None

                if t2 >= 0:
                    a2 = aoo[:, t2 % 2]

                    def _w_evac(v, t2=t2):
                        _wait(v, c_c, ev[t2 * NCH + NCH - 1])
                        if t2 >= 1:
                            _wait(v, st_lx, lx_st[t2 - 1])
                    emit_ln(blk, ps, a2, sqb[:], T, pre_sq=_w_evac)

                    @SEC(blk, "vector")
                    def _(v, t2=t2, a2=a2):
                        _wait(v, s_xb, xa_ld[t2])
                        if t2 >= 1:
                            _wait(v, st_lx, lx_st[t2 - 1])
                        v.tensor_tensor(sqb[:], a2, bc_ap(ab_bf[:, 0:T], NCH),
                                        ALU.mult)
                        v.tensor_tensor(a2, sqb[:], bc_ap(ab_bf[:, 512:1024], NCH),
                                        ALU.add)
                        return v.tensor_tensor(xa[:], xa[:], a2, ALU.add)
                    v3_done[t2] = v_c.n

                    @blk.sync
                    def _(s, t2=t2):
                        _wait(s, v_c, v3_done[t2])
                        _inc(s.dma_start(out=xmid[:, t2], in_=xa[:]), st_x, 16)
                    st_xm[t2] = st_x.n

                    def _cp3(v, t2=t2):
                        v.tensor_copy(sqb[:], xa[:])
                    emit_ln(blk, ps, sqb[:], aoo[:, t2 % 2], T, pre_sq=_cp3)

                    @SEC(blk, "vector")
                    def _(v, t2=t2, a2=a2):
                        v.tensor_tensor(a2, sqb[:], bc_ap(ab_bf[:, 0:T], NCH),
                                        ALU.mult)
                        return v.tensor_tensor(sqb[:], a2,
                                               bc_ap(ab_bf[:, 512:1024], NCH),
                                               ALU.add)
                    vlx_done[t2] = v_c.n

                    @blk.sync
                    def _(s, t2=t2):
                        _wait(s, v_c, vlx_done[t2])
                        _inc(s.dma_start(out=lx3[:, t2], in_=sqb[:]), st_lx, 16)
                    lx_st[t2] = st_lx.n

    G = 2
    NG = NIC // G
    with (
        nc.sbuf_tensor("c_xa", [128, NCH, T], F32) as xa2,
        nc.sbuf_tensor("c_ln3", [128, NCH, T], BF16) as ln3x,
        nc.sbuf_tensor("c_fuse", [128, NIC, T], BF16) as fuse,
        nc.sbuf_tensor("c_wgu", [128, 2, 2, G * D], BF16) as wgu_bf,
        nc.sbuf_tensor("c_wd", [128, 2, 4, D], BF16) as wd_bf,
        nc.sbuf_tensor("c_gb", [128, 4, T], BF16) as gb,
        nc.sbuf_tensor("c_ffo", [128, NCH, T], BF16) as ffo,
        nc.psum_tensor("c_ps", [128, 4096], F32) as ps,
    ):
        with nc.Block() as blk:
            block_guard(blk)
            sil_done = [None] * (NT * NG)
            upz_done = [None] * (NT * NG)
            dn_done = {}
            de_done = {}
            v3_done = [None] * NT
            st_out = [None] * NT
            g_wgu = [None] * (NT * NG)
            g_wd = {}
            gu_last = [None] * NT
            ln4_ab = [None] * NT

            for tt in range(NT):
                tc = slice(tt * T, (tt + 1) * T)

                @blk.sync
                def _(s, tt=tt, tc=tc):
                    _wait(s, st_x, st_xm[NT - 1] if tt == 0 else st_xm[tt])
                    if tt >= 1:
                        _wait(s, st_o, st_out[tt - 1])
                    _inc(s.dma_start(out=xa2[:], in_=xmid[:, tt]), s_xc, 16)
                xm_ld = s_xc.n

                if tt == 0:
                    @blk.sync
                    def _(s):
                        _wait(s, st_lx, lx_st[NT - 1])
                        _wait(s, st_x, st_xm[NT - 1])
                        _inc(s.dma_start(out=ln3x[:], in_=lx3[:, 0]), s_lx, 16)
                lx_ld = s_lx.n

                sil_jobs = []
                sil_base = c_c.n
                for grp in range(NG):
                    gi = tt * NG + grp
                    ic0 = grp * G
                    p2 = grp % 2

                    @blk.gpsimd
                    def _(g, tt=tt, grp=grp, gi=gi, ic0=ic0, p2=p2):
                        if gi == 0:
                            _wait(g, st_x, st_xm[NT - 1])
                            _wait(g, st_lx, lx_st[NT - 1])
                        if gi >= 2:
                            _wait(g, c_c, sil_done[gi - 2])
                            _wait(g, v_c, upz_done[gi - 2])
                        _inc(g.dma_start(out=wgu_bf[:, p2], in_=wgu_r[:, grp]),
                             g_g[p2], 16)
                    g_wgu[gi] = g_g[p2].n

                    @SEC(blk, "tensor")
                    def _(t, tt=tt, grp=grp, gi=gi, ic0=ic0, p2=p2):
                        _wait(t, g_g[p2], g_wgu[gi])
                        if grp <= 1:
                            _wait(t, s_lx, lx_ld)
                            if tt >= 1 and grp == 0:
                                _wait(t, v_b, de_done[(tt - 1, 1)])
                            if tt >= 1 and grp == 1:
                                _wait(t, v_c, ln4_ab[tt - 1])
                        if gi >= 2:
                            _wait(t, c_c, sil_done[gi - 2])
                            _wait(t, v_c, upz_done[gi - 2])
                        base = p2 * 2048
                        i = None
                        for j in range(G):
                            for c in range(NCH):
                                t.matmul(ps[:, base + j * 1024:base + j * 1024 + 512],
                                         wgu_bf[:, p2, 0, j * D + c * 128:
                                                j * D + (c + 1) * 128],
                                         ln3x[:, c, :],
                                         start=(c == 0), stop=(c == NCH - 1),
                                         skip_group_check=True)
                            for c in range(NCH):
                                i = t.matmul(ps[:, base + j * 1024 + 512:
                                                base + j * 1024 + 1024],
                                             wgu_bf[:, p2, 1, j * D + c * 128:
                                                   j * D + (c + 1) * 128],
                                             ln3x[:, c, :],
                                             start=(c == 0), stop=(c == NCH - 1),
                                             skip_group_check=True)
                        return i
                    t_gu = t_c.n
                    gu_last[tt] = t_gu

                    sil_jobs.append((grp, gi, ic0, p2, t_gu))
                    sil_done[gi] = sil_base + grp + 1

                    @SEC(blk, "vector")
                    def _(v, grp=grp, gi=gi, ic0=ic0, p2=p2, t_gu=t_gu):
                        _wait(v, t_c, t_gu)
                        _wait(v, c_c, sil_done[gi])
                        base = p2 * 2048
                        i = None
                        for j in range(G):
                            ic = ic0 + j
                            v.tensor_scalar_mul(fuse[:, ic, :],
                                                ps[:, base + j * 1024 + 512:
                                                   base + j * 1024 + 1024],
                                                su_sb[:, ic:ic + 1])
                            i = v.tensor_tensor(fuse[:, ic, :], gb[:, ic % 4],
                                                fuse[:, ic, :], ALU.mult)
                        return i
                    upz_done[gi] = v_c.n

                @SEC(blk, "scalar")
                def _(sc, sil_jobs=sil_jobs, tt=tt):
                    m = None
                    for grp, gi, ic0, p2, t_gu in sil_jobs:
                        _wait(sc, t_c, t_gu)
                        if gi == 0:
                            _wait(sc, s_sc, scales_ld)
                        base = p2 * 2048
                        for j in range(G):
                            ic = ic0 + j
                            m = sc.activation(gb[:, ic % 4],
                                              ps[:, base + j * 1024:
                                                 base + j * 1024 + 512],
                                              AF.Silu, scale=sg_sb[:, ic:ic + 1])
                        got = _inc(m, c_c, 1)
                        assert got == sil_done[tt * NG + grp], (tt, grp, got)
                    return None

                if tt + 1 < NT:
                    @SEC(blk, "scalar")
                    def _(sc, tt=tt):
                        _wait(sc, t_c, gu_last[tt])
                        _inc(sc.dma_start(out=ln3x[:], in_=lx3[:, tt + 1]),
                             s_lx, 16)
                        return None

                NGD = NIC // 4
                for half in range(2):
                    for gd in range(NGD):
                        ic0 = gd * 4
                        p2 = gd % 2

                        @blk.gpsimd
                        def _(g, tt=tt, half=half, gd=gd, ic0=ic0, p2=p2):
                            prev = None
                            if gd >= 2:
                                prev = (tt, half, gd - 2)
                            elif half == 1:
                                prev = (tt, 0, NGD - 2 + gd)
                            elif tt >= 1:
                                prev = (tt - 1, 1, NGD - 2 + gd)
                            if prev is not None:
                                _wait(g, t_b, dn_done[prev])
                            _inc(g.dma_start(
                                out=wd_bf[:, p2],
                                in_=wd_n[ic0 * 128:(ic0 + 4) * 128, :]
                                .rearrange("(a p) d -> p a d", p=128)),
                                 g_d[p2], 16)
                        g_wd[(tt, half, gd)] = g_d[p2].n

                        @SEC(blk, "tensor")
                        def _(t, tt=tt, half=half, gd=gd, ic0=ic0, p2=p2):
                            _wait(t, g_d[p2], g_wd[(tt, half, gd)])
                            if gd == 0:
                                _wait(t, v_c, upz_done[tt * NG + NG - 1])
                                if half == 0:
                                    _wait(t, c_c, sil_done[tt * NG + NG - 1])
                                else:
                                    _wait(t, v_b, de_done[(tt, 0)])
                            i = None
                            for j in range(4):
                                ic = ic0 + j
                                for m in range(8):
                                    mm = half * 8 + m
                                    i = t.matmul(ps[:, m * 512:(m + 1) * 512],
                                                 wd_bf[:, p2, j,
                                                       mm * 128:(mm + 1) * 128],
                                                 fuse[:, ic, :],
                                                 start=(ic == 0),
                                                 stop=(ic == NIC - 1),
                                                 skip_group_check=True)
                            _inc(i, t_b, 1)
                        dn_done[(tt, half, gd)] = t_b.n

                    @SEC(blk, "vector")
                    def _(v, tt=tt, half=half):
                        _wait(v, t_b, dn_done[(tt, half, NGD - 1)])
                        i = None
                        for m in range(8):
                            mm = half * 8 + m
                            i = v.tensor_scalar_mul(ffo[:, mm, :],
                                                    ps[:, m * 512:(m + 1) * 512],
                                                    sd_sb[:, mm:mm + 1])
                        _inc(i, v_b, 1)
                    de_done[(tt, half)] = v_b.n

                fv = fuse[:, 0:NCH, :]
                ln4_ab[tt] = emit_ln(blk, ps, ffo[:], fv, T)

                @SEC(blk, "vector")
                def _(v, tt=tt, fv=fv, xm_ld=xm_ld):
                    _wait(v, s_xc, xm_ld)
                    v.tensor_tensor(fv, ffo[:], bc_ap(ab_bf[:, 0:T], NCH), ALU.mult)
                    v.tensor_tensor(ffo[:], fv, bc_ap(ab_bf[:, 512:1024], NCH),
                                    ALU.add)
                    return v.tensor_tensor(xa2[:], xa2[:], ffo[:], ALU.add)
                v3_done[tt] = v_c.n

                @blk.sync
                def _(s, tt=tt, tc=tc):
                    _wait(s, v_c, v3_done[tt])
                    _inc(s.dma_start(out=outR[:, tt], in_=xa2[:]), st_o, 16)
                st_out[tt] = st_o.n

            @blk.sync
            def _(s):
                _wait(s, st_o, st_out[NT - 1])

    for sem, v in waits:
        assert v <= sem.n, f"wait {v} > final {sem.n} on {sem.h}"
    return nc


_CACHE = {}


def _prep_weights(wq, wk, wv, wo, wg, wu, wd, sg, su, sd):
    f32 = np.float32
    wq_r = np.ascontiguousarray(
        wq.T.reshape(NCH, 128, NH, 128).transpose(1, 2, 0, 3).reshape(128, -1),
        dtype=f32)
    wk_r = np.ascontiguousarray(
        wk.T.reshape(NCH, 128, 1024).transpose(1, 0, 2).reshape(128, -1), dtype=f32)
    wv_r = np.ascontiguousarray(
        wv.T.reshape(NCH, 128, 1024).transpose(1, 0, 2).reshape(128, -1), dtype=f32)
    wo_r = np.ascontiguousarray(
        wo.T.reshape(NCH, 128, D).transpose(1, 0, 2).reshape(128, -1), dtype=f32)
    wg_r = wg.reshape(NCH, 128, NIC, 128).transpose(1, 2, 0, 3).reshape(128, NIC // 2, 2 * D)
    wu_r = wu.reshape(NCH, 128, NIC, 128).transpose(1, 2, 0, 3).reshape(128, NIC // 2, 2 * D)
    wgu_r = np.ascontiguousarray(np.stack([wg_r, wu_r], axis=2))
    wd_n = np.ascontiguousarray(wd)
    sg_r = np.ascontiguousarray(sg.reshape(NIC, 128).T, dtype=f32)
    su_r = np.ascontiguousarray(su.reshape(NIC, 128).T, dtype=f32)
    sd_r = np.ascontiguousarray(sd.reshape(NCH, 128).T, dtype=f32)
    return dict(wq_r=wq_r, wk_r=wk_r, wv_r=wv_r, wo_r=wo_r, wgu_r=wgu_r,
                wd_n=wd_n, sg_r=sg_r, su_r=su_r, sd_r=sd_r)


def kernel(L0_attn_hidden_states, L0_attn_wq, L0_attn_wk, L0_attn_wv,
           L0_attn_wo, L0_ffn_w_gate, L0_ffn_w_up, L0_ffn_w_down,
           L0_ffn_s_gate, L0_ffn_s_up, L0_ffn_s_down):
    x = np.asarray(L0_attn_hidden_states, dtype=np.float32)
    B = x.shape[0]
    shared = _prep_weights(
        np.asarray(L0_attn_wq, dtype=np.float32),
        np.asarray(L0_attn_wk, dtype=np.float32),
        np.asarray(L0_attn_wv, dtype=np.float32),
        np.asarray(L0_attn_wo, dtype=np.float32),
        np.asarray(L0_ffn_w_gate, dtype=np.int8),
        np.asarray(L0_ffn_w_up, dtype=np.int8),
        np.asarray(L0_ffn_w_down, dtype=np.int8),
        np.asarray(L0_ffn_s_gate, dtype=np.float32),
        np.asarray(L0_ffn_s_up, dtype=np.float32),
        np.asarray(L0_ffn_s_down, dtype=np.float32))

    if "nc" not in _CACHE:
        _CACHE["nc"] = build_program()
    nc = _CACHE["nc"]

    in_maps = []
    for b in range(B):
        m = dict(shared)
        m["xR"] = np.ascontiguousarray(
            x[b].T.reshape(NCH, 128, NT, T).transpose(1, 2, 0, 3))
        in_maps.append(m)

    res = run_bass_kernel_spmd(nc, in_maps, core_ids=list(range(B)))
    out = np.stack([np.asarray(res.results[b]["outR"]).transpose(1, 3, 2, 0)
                    .reshape(S, D) for b in range(B)], axis=0)
    return np.ascontiguousarray(out, dtype=np.float32)


# version 13
# speedup vs baseline: 1.0850x; 1.0850x over previous
import numpy as np
import concourse.bass as bass
import concourse.mybir as mybir
from concourse.ap import AP
from concourse.bass_utils import run_bass_kernel_spmd

F32 = mybir.dt.float32
BF16 = mybir.dt.bfloat16
I8 = mybir.dt.int8
AF = mybir.ActivationFunctionType
ALU = mybir.AluOpType

D = 2048
S = 2048
NCH = 16
NH = 16
NKV = 8
INTER = 8192
NIC = 64
EPS = 1e-5
ISQ = float(np.float32(1.0) / np.sqrt(np.float32(128.0)))
T = 512
NT = S // T
TA = 512
NTA = S // TA


def bc_ap(ap2d, reps):
    return AP(tensor=ap2d.tensor, offset=ap2d.offset,
              ap=[list(ap2d.ap[0]), [0, reps], list(ap2d.ap[1])])


class Sem:
    __slots__ = ("h", "n")

    def __init__(self, stack, nc, name):
        self.h = stack.enter_context(nc.semaphore(name))
        self.n = 0


def build_program():
    from contextlib import ExitStack
    nc = bass.Bass("TRN2", target_bir_lowering=False,
                   detect_race_conditions=False)
    waits = []

    xR = nc.dram_tensor("xR", [128, NT, NCH, T], F32, kind="ExternalInput").ap()
    wq_r = nc.dram_tensor("wq_r", [128, NH * D], F32, kind="ExternalInput").ap()
    wk_r = nc.dram_tensor("wk_r", [128, NCH * 1024], F32, kind="ExternalInput").ap()
    wv_r = nc.dram_tensor("wv_r", [128, NCH * 1024], F32, kind="ExternalInput").ap()
    wo_r = nc.dram_tensor("wo_r", [128, NCH * D], F32, kind="ExternalInput").ap()
    wgu_r = nc.dram_tensor("wgu_r", [128, NIC // 2, 2, 2 * D], I8,
                           kind="ExternalInput").ap()
    wd_n = nc.dram_tensor("wd_n", [INTER, D], I8, kind="ExternalInput").ap()
    sg_r = nc.dram_tensor("sg_r", [128, NIC], F32, kind="ExternalInput").ap()
    su_r = nc.dram_tensor("su_r", [128, NIC], F32, kind="ExternalInput").ap()
    sd_r = nc.dram_tensor("sd_r", [128, NCH], F32, kind="ExternalInput").ap()
    import os
    _dbg = os.environ.get("K_DBG_SCRATCH") == "1"
    outR = nc.dram_tensor("outR", [128, NT, NCH, T], F32, kind="ExternalOutput").ap()
    attno = nc.dram_tensor("attno", [128, NT, NCH, T], BF16,
                           kind="ExternalOutput" if _dbg else "Internal").ap()
    xmid = nc.dram_tensor("xmid", [128, NT, NCH, T], F32,
                          kind="ExternalOutput" if _dbg else "Internal").ap()
    lx3 = nc.dram_tensor("lx3", [128, NT, NCH, T], BF16, kind="Internal").ap()


    stack = nc.ctx

    s_sc = Sem(stack, nc, "s_sc")
    s_x = [Sem(stack, nc, "s_x0"), Sem(stack, nc, "s_x1")]
    s_xb = Sem(stack, nc, "s_xb")
    s_xc = Sem(stack, nc, "s_xc")
    st_a = [Sem(stack, nc, "st_a0"), Sem(stack, nc, "st_a1")]
    st_x = Sem(stack, nc, "st_x")
    st_o = Sem(stack, nc, "st_o")
    st_lx = Sem(stack, nc, "st_lx")
    s_lx = Sem(stack, nc, "s_lx")
    g_wk = Sem(stack, nc, "g_wk")
    g_wv = Sem(stack, nc, "g_wv")
    g_wo = Sem(stack, nc, "g_wo")
    g_q = [Sem(stack, nc, "g_q0"), Sem(stack, nc, "g_q1")]
    g_g = [Sem(stack, nc, "g_g0"), Sem(stack, nc, "g_g1")]
    g_u = [Sem(stack, nc, "g_u0"), Sem(stack, nc, "g_u1")]
    g_d = [Sem(stack, nc, "g_d0"), Sem(stack, nc, "g_d1")]
    t_c = Sem(stack, nc, "t_c")
    v_c = Sem(stack, nc, "v_c")
    c_c = Sem(stack, nc, "c_c")
    t_b = Sem(stack, nc, "t_b")
    v_b = Sem(stack, nc, "v_b")
    c_b = Sem(stack, nc, "c_b")
    COMPUTE_SEMS = (t_c, v_c, c_c, t_b, v_b, c_b)
    ENG_SEMS = {"tensor": (t_c, t_b), "vector": (v_c, v_b), "scalar": (c_c, c_b)}

    def _inc(inst, sem, k=1):
        inst.then_inc(sem.h, k)
        sem.n += k
        return sem.n

    def _wait(eng, sem, v):
        if v <= 0:
            return
        waits.append((sem, v))
        eng.wait_ge(sem.h, v)

    def SEC(blk, name):
        def deco(body):
            dec = getattr(blk, name)

            @dec
            def _(eng):
                pr = ENG_SEMS.get(name)
                if pr:
                    _wait(eng, pr[0], pr[0].n)
                    _wait(eng, pr[1], pr[1].n)
                last = body(eng)
                if pr and last is not None:
                    _inc(last, pr[0], 1)
            return None
        return deco

    def block_guard(blk):
        for name in ("tensor", "vector", "scalar"):
            dec = getattr(blk, name)
            snap = [(s, s.n) for s in COMPUTE_SEMS]

            @dec
            def _(eng, snap=snap):
                for s, v in snap:
                    _wait(eng, s, v)

    ones_bf = stack.enter_context(nc.sbuf_tensor("ones_bf", [128, 128], BF16))
    ones_f = stack.enter_context(nc.sbuf_tensor("ones_f", [128, 128], F32))
    sg_sb = stack.enter_context(nc.sbuf_tensor("sg_sb", [128, NIC], F32))
    su_sb = stack.enter_context(nc.sbuf_tensor("su_sb", [128, NIC], F32))
    sd_sb = stack.enter_context(nc.sbuf_tensor("sd_sb", [128, NCH], F32))
    lnm = stack.enter_context(nc.sbuf_tensor("lnm", [128, 512], F32))
    lnv = stack.enter_context(nc.sbuf_tensor("lnv", [128, 512], F32))
    lnt = stack.enter_context(nc.sbuf_tensor("lnt", [128, 512], F32))
    ab_bf = stack.enter_context(nc.sbuf_tensor("ab_bf", [128, 1024], BF16))


    res_scope = ExitStack()
    ln1x = res_scope.enter_context(nc.sbuf_tensor("ln1x", [128, NCH, S], BF16))
    K_sb = res_scope.enter_context(nc.sbuf_tensor("K_sb", [128, NKV, S], BF16))
    V_sb = res_scope.enter_context(nc.sbuf_tensor("V_sb", [128, NCH, 1024], BF16))

    a_scope = ExitStack()
    wk_bf = a_scope.enter_context(nc.sbuf_tensor("a2_wk", [128, NCH, 1024], BF16))

    with nc.Block() as blk:
        @SEC(blk, "vector")
        def _(v):
            v.memset(ones_bf[:], 1.0)
            return v.memset(ones_f[:], 1.0)

        @blk.gpsimd
        def _(g):
            _inc(g.dma_start(out=wk_bf[:],
                             in_=wk_r.rearrange("p (c k) -> p c k", c=NCH)),
                 g_wk, 16)

        @blk.sync
        def _(s):
            _inc(s.dma_start(out=sg_sb[:], in_=sg_r[:]), s_sc, 16)
            _wait(s, s_sc, 16)
            _inc(s.dma_start(out=su_sb[:], in_=su_r[:]), s_sc, 16)
            _wait(s, s_sc, 32)
            _inc(s.dma_start(out=sd_sb[:], in_=sd_r[:]), s_sc, 16)
    scales_ld = s_sc.n
    wk_rdy = g_wk.n

    def emit_ln(blk, ps, in_bf, sq_out, Tt, pre_sq=None):
        @SEC(blk, "vector")
        def _(v):
            if pre_sq is not None:
                pre_sq(v)
            return v.tensor_tensor(sq_out, in_bf, in_bf, ALU.mult)
        v_sq = v_c.n

        @SEC(blk, "tensor")
        def _(t):
            _wait(t, v_c, v_sq)
            for c in range(NCH):
                t.matmul(ps[:, 2048:2048 + Tt], ones_bf[:, 0:128],
                         in_bf[:, c, :], start=(c == 0), stop=(c == NCH - 1),
                         skip_group_check=True)
            i = None
            for c in range(NCH):
                i = t.matmul(ps[:, 2560:2560 + Tt], ones_bf[:, 0:128],
                             sq_out[:, c, :], start=(c == 0), stop=(c == NCH - 1),
                             skip_group_check=True)
            return i
        t_sums = t_c.n

        @SEC(blk, "vector")
        def _(v):
            _wait(v, t_c, t_sums)
            v.tensor_scalar_mul(lnm[:, 0:Tt], ps[:, 2048:2048 + Tt], 1.0 / D)
            v.tensor_scalar_mul(lnv[:, 0:Tt], ps[:, 2560:2560 + Tt], 1.0 / D)
            v.tensor_tensor(lnt[:, 0:Tt], lnm[:, 0:Tt], lnm[:, 0:Tt], ALU.mult)
            v.tensor_tensor(lnv[:, 0:Tt], lnv[:, 0:Tt], lnt[:, 0:Tt], ALU.subtract)
            v.tensor_scalar_add(lnv[:, 0:Tt], lnv[:, 0:Tt], EPS)
            return v.reciprocal(lnv[:, 0:Tt], lnv[:, 0:Tt])
        v_rv = v_c.n

        @SEC(blk, "scalar")
        def _(sc):
            _wait(sc, v_c, v_rv)
            return sc.activation(lnv[:, 0:Tt], lnv[:, 0:Tt], AF.Sqrt)
        c_sqrt = c_c.n

        @SEC(blk, "vector")
        def _(v):
            _wait(v, c_c, c_sqrt)
            v.tensor_copy(ab_bf[:, 0:Tt], lnv[:, 0:Tt])
            return v.scalar_tensor_tensor(ab_bf[:, 512:512 + Tt], lnm[:, 0:Tt],
                                          -1.0, lnv[:, 0:Tt], ALU.mult, ALU.mult)
        return v_c.n

    with (
        nc.sbuf_tensor("a1_xbf", [128, NCH, TA], BF16) as xbf,
        nc.psum_tensor("a1_ps", [128, 4096], F32) as ps,
    ):
        with nc.Block() as blk:
            block_guard(blk)
            xa_ld = [None] * NTA
            v_nm = [None] * NTA

            @blk.gpsimd
            def _(g):
                _inc(g.dma_start(out=xbf[:], in_=xR[:, 0]), g_g[0], 16)
            xa_ld[0] = g_g[0].n

            for ti in range(NTA):
                t0 = ti * TA
                lslice = ln1x[:, :, t0:t0 + TA]

                def _wx(v, ti=ti):
                    _wait(v, g_g[0], xa_ld[ti])
                emit_ln(blk, ps, xbf[:], lslice, TA, pre_sq=_wx)

                @SEC(blk, "vector")
                def _(v, ti=ti, lslice=lslice):
                    v.tensor_tensor(lslice, xbf[:], bc_ap(ab_bf[:, 0:TA], NCH),
                                    ALU.mult)
                    return v.tensor_tensor(lslice, lslice,
                                           bc_ap(ab_bf[:, 512:512 + TA], NCH),
                                           ALU.add)
                v_nm[ti] = v_c.n

                if ti + 1 < NTA:
                    @blk.gpsimd
                    def _(g, ti=ti):
                        _wait(g, v_c, v_nm[ti])
                        _inc(g.dma_start(out=xbf[:], in_=xR[:, ti + 1]),
                             g_g[0], 16)
                    xa_ld[ti + 1] = g_g[0].n

    with (
        nc.sbuf_tensor("a2_wv", [128, NCH, 1024], BF16) as wv_bf,
        nc.psum_tensor("a2_ps", [128, 4096], F32) as ps,
    ):
        with nc.Block() as blk:
            block_guard(blk)

            @blk.gpsimd
            def _(g):
                _inc(g.dma_start(out=wv_bf[:],
                                 in_=wv_r.rearrange("p (c k) -> p c k", c=NCH)),
                     g_wv, 16)
            wv_rdy = g_wv.n

            ev = [None] * (NKV * NT)
            for ckv in range(NKV):
                for tt in range(NT):
                    it = ckv * NT + tt

                    @SEC(blk, "tensor")
                    def _(t, ckv=ckv, tt=tt, it=it):
                        if it == 0:
                            _wait(t, g_wk, wk_rdy)
                        if it >= 2:
                            _wait(t, v_c, ev[it - 2])
                        pcol = (it % 2) * 512
                        i = None
                        for c in range(NCH):
                            i = t.matmul(ps[:, pcol:pcol + 512],
                                         wk_bf[:, c, ckv * 128:(ckv + 1) * 128],
                                         ln1x[:, c, tt * T:(tt + 1) * T],
                                         start=(c == 0), stop=(c == NCH - 1),
                                         skip_group_check=True)
                        return i
                    tk = t_c.n

                    @SEC(blk, "vector")
                    def _(v, ckv=ckv, tt=tt, it=it, tk=tk):
                        _wait(v, t_c, tk)
                        pcol = (it % 2) * 512
                        return v.tensor_copy(K_sb[:, ckv, tt * T:(tt + 1) * T],
                                             ps[:, pcol:pcol + 512])
                    ev[it] = v_c.n

            evv = [None] * NCH
            for ck in range(NCH):
                @SEC(blk, "tensor")
                def _(t, ck=ck):
                    if ck == 0:
                        _wait(t, g_wv, wv_rdy)
                        _wait(t, v_c, ev[NKV * NT - 1])
                    if ck >= 2:
                        _wait(t, c_c, evv[ck - 2])
                    pcol = (ck % 2) * 1024
                    i = None
                    for c in range(NCH):
                        for nh in range(2):
                            i = t.matmul(
                                ps[:, pcol + nh * 512:pcol + nh * 512 + 512],
                                ln1x[:, c, ck * 128:(ck + 1) * 128],
                                wv_bf[:, c, nh * 512:(nh + 1) * 512],
                                start=(c == 0), stop=(c == NCH - 1),
                                skip_group_check=True)
                    return i
                tv = t_c.n

                @SEC(blk, "scalar")
                def _(sc, ck=ck, tv=tv):
                    _wait(sc, t_c, tv)
                    pcol = (ck % 2) * 1024
                    return sc.copy(V_sb[:, ck, :], ps[:, pcol:pcol + 1024])
                evv[ck] = c_c.n

    a_scope.close()

    with (
        nc.sbuf_tensor("b1_wq", [128, 2, D], BF16) as wq_bf,
        nc.sbuf_tensor("b1_qbf", [128, 2, T], BF16) as qbf,
        nc.sbuf_tensor("b1_probs", [128, 2, NCH * T], BF16) as probs,
        nc.sbuf_tensor("b1_obf", [128, 2, T], BF16) as obf,
        nc.sbuf_tensor("b1_osc", [128, 2, T], BF16) as osc,
        nc.sbuf_tensor("b1_rec", [128, T], F32) as rec,
        nc.sbuf_tensor("b1_tree", [128, 2, 8, T], BF16) as tree,
        nc.psum_tensor("b1_ps", [128, 4096], F32) as ps,
    ):
        PO, PQS = 3072, 3584
        with nc.Block() as blk:
            block_guard(blk)

            n_it = NH * NT
            wq_ld = [None] * NH
            tq_done = [None] * (n_it + 1)
            vq_done = [None] * (n_it + 1)
            sc_exp = [None] * (n_it * 8)
            tpv_done = [None] * n_it
            sums_done = [None] * n_it
            tree_done = [None] * n_it
            rc_done = [None] * n_it
            obf_done = [None] * n_it
            osc_done = [None] * n_it
            st_ao = [None] * n_it

            @blk.gpsimd
            def _(g):
                _inc(g.dma_start(out=wq_bf[:, 0], in_=wq_r[:, 0:D]), g_q[0], 16)
            wq_ld[0] = g_q[0].n

            def emit_q(t, j):
                h, tt = j // NT, j % NT
                if tt == 0:
                    _wait(t, g_q[h % 2], wq_ld[h])
                if j >= 3:
                    _wait(t, v_b, rc_done[j - 3])
                _wait(t, v_c, vq_done[j - 1] if j >= 1 else 0)
                i = None
                for c in range(NCH):
                    i = t.matmul(ps[:, PQS:PQS + 512],
                                 wq_bf[:, h % 2, c * 128:(c + 1) * 128],
                                 ln1x[:, c, tt * T:(tt + 1) * T],
                                 start=(c == 0), stop=(c == NCH - 1),
                                 skip_group_check=True)
                return i

            @SEC(blk, "tensor")
            def _(t):
                return emit_q(t, 0)
            tq_done[0] = t_c.n

            @SEC(blk, "vector")
            def _(v):
                _wait(v, t_c, tq_done[0])
                return v.tensor_copy(qbf[:, 0], ps[:, PQS:PQS + 512])
            vq_done[0] = v_c.n

            C_base = c_c.n
            exp_jobs = []
            for g2 in range(n_it * 8):
                sc_exp[g2] = C_base + g2 + 1

            for i in range(n_it):
                h, tt = i // NT, i % NT
                kvh = h // 2

                if tt == 0 and h + 1 < NH:
                    @blk.gpsimd
                    def _(g, h=h):
                        _wait(g, t_c, tq_done[h * NT - 1] if h >= 1 else 0)
                        _inc(g.dma_start(out=wq_bf[:, (h + 1) % 2],
                                         in_=wq_r[:, (h + 1) * D:(h + 2) * D]),
                             g_q[(h + 1) % 2], 16)
                    wq_ld[h + 1] = g_q[(h + 1) % 2].n

                c0 = C_base + 8 * i

                @SEC(blk, "tensor")
                def _(t, i=i, kvh=kvh, c0=c0):
                    _wait(t, v_c, vq_done[i])
                    for g in range(8):
                        _wait(t, c_c, c0 + g - 2)
                        base = (g % 3) * 1024
                        for j in range(2):
                            kc = 2 * g + j
                            m = t.matmul(
                                ps[:, base + j * 512:base + j * 512 + 512],
                                K_sb[:, kvh, kc * 128:(kc + 1) * 128],
                                qbf[:, i % 2],
                                start=True, stop=True, skip_group_check=True)
                        _inc(m, t_c, 1)
                t_sc_base = t_c.n - 8

                if i + 1 < n_it:
                    @SEC(blk, "tensor")
                    def _(t, i=i):
                        return emit_q(t, i + 1)
                    tq_done[i + 1] = t_c.n

                exp_jobs.append((i, t_sc_base))

                if i + 1 < n_it:
                    @SEC(blk, "vector")
                    def _(v, i=i):
                        _wait(v, t_c, tq_done[i + 1])
                        return v.tensor_copy(qbf[:, (i + 1) % 2],
                                             ps[:, PQS:PQS + 512])
                    vq_done[i + 1] = v_c.n

                @SEC(blk, "tensor")
                def _(t, i=i, kvh=kvh):
                    if i >= 1:
                        _wait(t, v_c, vq_done[min(i + 1, n_it - 1)])
                        _wait(t, v_c, tree_done[i - 1])
                        m = t.matmul(ps[:, PQS:PQS + 512], ones_bf[:, 0:128],
                                     tree[:, (i - 1) % 2, 0, :],
                                     start=True, stop=True, skip_group_check=True)
                        sums_done[i - 1] = _inc(m, t_b, 1)
                    if i >= 1:
                        _wait(t, v_b, obf_done[i - 1])
                    for g in range(8):
                        _wait(t, c_c, sc_exp[i * 8 + g])
                        for j in range(2):
                            kc = 2 * g + j
                            m = t.matmul(ps[:, PO:PO + 512],
                                         V_sb[:, kc, kvh * 128:(kvh + 1) * 128],
                                         probs[:, i % 2, kc * T:(kc + 1) * T],
                                         start=(kc == 0), stop=(kc == NCH - 1),
                                         skip_group_check=True)
                    tpv_done[i] = _inc(m, t_b, 1)

                @SEC(blk, "vector")
                def _(v, i=i):
                    _wait(v, c_c, sc_exp[i * 8 + 7])
                    if i >= 2:
                        _wait(v, t_b, sums_done[i - 2])
                    tr = tree[:, i % 2]
                    v.tensor_tensor(tr[:, :, :],
                                    probs[:, i % 2, 0:8 * T].rearrange(
                                        "p (a t) -> p a t", a=8),
                                    probs[:, i % 2, 8 * T:16 * T].rearrange(
                                        "p (a t) -> p a t", a=8), ALU.add)
                    v.tensor_tensor(tr[:, 0:4, :], tr[:, 0:4, :], tr[:, 4:8, :],
                                    ALU.add)
                    v.tensor_tensor(tr[:, 0:2, :], tr[:, 0:2, :], tr[:, 2:4, :],
                                    ALU.add)
                    return v.tensor_tensor(tr[:, 0:1, :], tr[:, 0:1, :],
                                           tr[:, 1:2, :], ALU.add)
                tree_done[i] = v_c.n

                @SEC(blk, "vector")
                def _(v, i=i):
                    if i >= 2:
                        if i >= 4:
                            _wait(v, st_a[i % 2], st_ao[i - 4])
                        m = v.tensor_tensor(osc[:, i % 2], obf[:, i % 2],
                                            rec[:], ALU.mult)
                        osc_done[i - 2] = _inc(m, v_b, 1)
                    _wait(v, t_b, tpv_done[i])
                    m = v.tensor_copy(obf[:, i % 2], ps[:, PO:PO + 512])
                    obf_done[i] = _inc(m, v_b, 1)
                    if i >= 1:
                        _wait(v, t_b, sums_done[i - 1])
                        m = v.reciprocal(rec[:], ps[:, PQS:PQS + 512])
                        rc_done[i - 1] = _inc(m, v_b, 1)
                    return None

                if i >= 2:
                    @blk.sync
                    def _(s, i=i):
                        j = i - 2
                        hh, ttt = j // NT, j % NT
                        _wait(s, v_b, osc_done[j])
                        _inc(s.dma_start(out=attno[:, ttt, hh, :],
                                         in_=osc[:, j % 2]), st_a[j % 2], 16)
                    st_ao[i - 2] = st_a[i % 2].n

            @SEC(blk, "scalar")
            def _(sc):
                for i2, tsb in exp_jobs:
                    if i2 >= 2:
                        _wait(sc, t_b, tpv_done[i2 - 2])
                    for g in range(8):
                        _wait(sc, t_c, tsb + g + 1)
                        base = (g % 3) * 1024
                        m = sc.activation(
                            probs[:, i2 % 2, 2 * g * T:(2 * g + 2) * T],
                            ps[:, base:base + 1024], AF.Exp, scale=ISQ)
                        got = _inc(m, c_c, 1)
                        assert got == sc_exp[i2 * 8 + g], (i2, g, got)
                return None

            for i in (n_it, n_it + 1):
                @SEC(blk, "tensor")
                def _(t, i=i):
                    if i - 1 < n_it:
                        _wait(t, v_b, rc_done[i - 2])
                        _wait(t, v_c, tree_done[i - 1])
                        m = t.matmul(ps[:, PQS:PQS + 512], ones_bf[:, 0:128],
                                     tree[:, (i - 1) % 2, 0, :],
                                     start=True, stop=True, skip_group_check=True)
                        sums_done[i - 1] = _inc(m, t_b, 1)
                        return None
                    return None

                @SEC(blk, "vector")
                def _(v, i=i):
                    _wait(v, st_a[i % 2], st_ao[i - 4])
                    m = v.tensor_tensor(osc[:, i % 2], obf[:, (i - 2) % 2],
                                        rec[:], ALU.mult)
                    osc_done[i - 2] = _inc(m, v_b, 1)
                    if i - 1 < n_it:
                        _wait(v, t_b, sums_done[i - 1])
                        m = v.reciprocal(rec[:], ps[:, PQS:PQS + 512])
                        rc_done[i - 1] = _inc(m, v_b, 1)
                    return None

                @blk.sync
                def _(s, i=i):
                    j = i - 2
                    hh, ttt = j // NT, j % NT
                    _wait(s, v_b, osc_done[j])
                    _inc(s.dma_start(out=attno[:, ttt, hh, :],
                                     in_=osc[:, j % 2]), st_a[j % 2], 16)
                st_ao[i - 2] = st_a[i % 2].n

            b1_store_total = (st_a[0].n, st_a[1].n)

    res_scope.close()

    st_xm = [None] * NT
    lx_st = [None] * NT
    with (
        nc.sbuf_tensor("b2_wo", [128, NCH, D], BF16) as wo_bf,
        nc.sbuf_tensor("b2_ao", [128, 2, NCH, T], BF16) as ao,
        nc.sbuf_tensor("b2_xa", [128, NCH, T], F32) as xa,
        nc.sbuf_tensor("b2_aoo", [128, 2, NCH, T], BF16) as aoo,
        nc.sbuf_tensor("b2_sq", [128, NCH, T], BF16) as sqb,
        nc.psum_tensor("b2_ps", [128, 4096], F32) as ps,
    ):
        with nc.Block() as blk:
            block_guard(blk)

            @blk.gpsimd
            def _(g):
                _wait(g, st_a[0], b1_store_total[0])
                _wait(g, st_a[1], b1_store_total[1])
                _inc(g.dma_start(out=wo_bf[:],
                                 in_=wo_r.rearrange("p (c o) -> p c o", c=NCH)),
                     g_wo, 16)
            wo_rdy = g_wo.n

            ao_ld = []

            @blk.sync
            def _(s):
                _wait(s, st_a[0], b1_store_total[0])
                _wait(s, st_a[1], b1_store_total[1])
                _inc(s.dma_start(out=ao[:, 0], in_=attno[:, 0]), s_x[0], 16)
                ao_ld.append(s_x[0].n)

            ev = [None] * (NT * NCH)
            twm = {}
            v3_done = [None] * NT
            vlx_done = [None] * NT
            xa_ld = [None] * NT
            for tt in range(NT + 1):
                t2 = tt - 1

                if tt < NT and tt + 1 < NT:
                    @blk.sync
                    def _(s, tt=tt):
                        if tt >= 1:
                            _wait(s, c_c, ev[(tt - 1) * NCH + NCH - 1])
                        _inc(s.dma_start(out=ao[:, (tt + 1) % 2],
                                         in_=attno[:, tt + 1]),
                             s_x[(tt + 1) % 2], 16)
                        ao_ld.append(s_x[(tt + 1) % 2].n)

                if t2 >= 0:
                    @blk.sync
                    def _(s, t2=t2):
                        if t2 >= 1:
                            _wait(s, v_c, vlx_done[t2 - 1])
                            _wait(s, st_x, st_xm[t2 - 1])
                        _inc(s.dma_start(out=xa[:], in_=xR[:, t2]), s_xb, 16)
                    xa_ld[t2] = s_xb.n

                if tt < NT:
                    e0 = c_c.n
                    for m in range(NCH):
                        ev[tt * NCH + m] = e0 + m + 1 + \
                            (1 if (t2 >= 0 and m >= 2) else 0)

                a2 = aoo[:, t2 % 2] if t2 >= 0 else None

                if t2 >= 0:
                    @SEC(blk, "vector")
                    def _(v, t2=t2, a2=a2):
                        _wait(v, c_c, ev[t2 * NCH + NCH - 1])
                        if t2 >= 1:
                            _wait(v, st_lx, lx_st[t2 - 1])
                        return v.tensor_tensor(sqb[:], a2, a2, ALU.mult)
                    v_sq2 = v_c.n

                def _wo(t, tt, m):
                    if m == 0:
                        _wait(t, g_wo, wo_rdy)
                        _wait(t, s_x[tt % 2], ao_ld[tt])
                    mi = tt * NCH + m
                    if mi >= 2:
                        _wait(t, c_c, ev[mi - 2])
                    pcol = (m % 2) * 512
                    i = None
                    for c in range(NCH):
                        i = t.matmul(ps[:, pcol:pcol + 512],
                                     wo_bf[:, c, m * 128:(m + 1) * 128],
                                     ao[:, tt % 2, c, :],
                                     start=(c == 0), stop=(c == NCH - 1),
                                     skip_group_check=True)
                    return i

                def _stats(v, twait):
                    _wait(v, t_c, twait)
                    v.tensor_scalar_mul(lnm[:, 0:T], ps[:, 2048:2048 + T],
                                        1.0 / D)
                    v.tensor_scalar_mul(lnv[:, 0:T], ps[:, 2560:2560 + T],
                                        1.0 / D)
                    v.tensor_tensor(lnt[:, 0:T], lnm[:, 0:T], lnm[:, 0:T],
                                    ALU.mult)
                    v.tensor_tensor(lnv[:, 0:T], lnv[:, 0:T], lnt[:, 0:T],
                                    ALU.subtract)
                    v.tensor_scalar_add(lnv[:, 0:T], lnv[:, 0:T], EPS)
                    return v.reciprocal(lnv[:, 0:T], lnv[:, 0:T])

                def _ab(v, cwait):
                    _wait(v, c_c, cwait)
                    v.tensor_copy(ab_bf[:, 0:T], lnv[:, 0:T])
                    return v.scalar_tensor_tensor(ab_bf[:, 512:512 + T],
                                                  lnm[:, 0:T], -1.0,
                                                  lnv[:, 0:T], ALU.mult,
                                                  ALU.mult)

                if tt < NT:
                    for m in (0, 1):
                        @SEC(blk, "tensor")
                        def _(t, tt=tt, m=m):
                            return _wo(t, tt, m)
                        twm[tt * NCH + m] = t_c.n

                if t2 >= 0:
                    @SEC(blk, "tensor")
                    def _(t, v_sq2=v_sq2, a2=a2):
                        _wait(t, v_c, v_sq2)
                        for c in range(NCH):
                            t.matmul(ps[:, 2048:2048 + T], ones_bf[:, 0:128],
                                     a2[:, c, :], start=(c == 0),
                                     stop=(c == NCH - 1),
                                     skip_group_check=True)
                        i = None
                        for c in range(NCH):
                            i = t.matmul(ps[:, 2560:2560 + T],
                                         ones_bf[:, 0:128], sqb[:, c, :],
                                         start=(c == 0), stop=(c == NCH - 1),
                                         skip_group_check=True)
                        return i
                    t_sums2 = t_c.n

                    @SEC(blk, "vector")
                    def _(v, t_sums2=t_sums2):
                        return _stats(v, t_sums2)
                    v_rv2 = v_c.n

                if tt < NT:
                    @SEC(blk, "scalar")
                    def _(sc, tt=tt):
                        m2 = None
                        for m in (0, 1):
                            _wait(sc, t_c, twm[tt * NCH + m])
                            if m == 0 and tt >= 2:
                                _wait(sc, v_c, vlx_done[tt - 2])
                            m2 = sc.copy(aoo[:, tt % 2, m, :],
                                         ps[:, (m % 2) * 512:
                                            (m % 2) * 512 + 512])
                            got = _inc(m2, c_c, 1)
                            assert got == ev[tt * NCH + m], (tt, m, got)
                        return None

                if t2 >= 0:
                    @SEC(blk, "scalar")
                    def _(sc, v_rv2=v_rv2):
                        _wait(sc, v_c, v_rv2)
                        return sc.activation(lnv[:, 0:T], lnv[:, 0:T], AF.Sqrt)
                    c_sqrt2 = c_c.n
                    if tt < NT:
                        assert c_sqrt2 == e0 + 3, (tt, c_sqrt2, e0)

                    @SEC(blk, "vector")
                    def _(v, c_sqrt2=c_sqrt2):
                        return _ab(v, c_sqrt2)

                    @SEC(blk, "vector")
                    def _(v, t2=t2, a2=a2):
                        _wait(v, s_xb, xa_ld[t2])
                        v.tensor_tensor(sqb[:], a2, bc_ap(ab_bf[:, 0:T], NCH),
                                        ALU.mult)
                        v.tensor_tensor(a2, sqb[:],
                                        bc_ap(ab_bf[:, 512:1024], NCH),
                                        ALU.add)
                        return v.tensor_tensor(xa[:], xa[:], a2, ALU.add)
                    v3_done[t2] = v_c.n

                    @blk.sync
                    def _(s, t2=t2):
                        _wait(s, v_c, v3_done[t2])
                        _inc(s.dma_start(out=xmid[:, t2], in_=xa[:]), st_x, 16)
                    st_xm[t2] = st_x.n

                    @SEC(blk, "vector")
                    def _(v, a2=a2):
                        v.tensor_copy(sqb[:], xa[:])
                        return v.tensor_tensor(a2, sqb[:], sqb[:], ALU.mult)
                    v_sq3 = v_c.n

                if tt < NT:
                    for m in range(2, 14):
                        @SEC(blk, "tensor")
                        def _(t, tt=tt, m=m):
                            return _wo(t, tt, m)
                        twm[tt * NCH + m] = t_c.n

                if t2 >= 0:
                    @SEC(blk, "tensor")
                    def _(t, v_sq3=v_sq3, a2=a2):
                        _wait(t, v_c, v_sq3)
                        for c in range(NCH):
                            t.matmul(ps[:, 2048:2048 + T], ones_bf[:, 0:128],
                                     sqb[:, c, :], start=(c == 0),
                                     stop=(c == NCH - 1),
                                     skip_group_check=True)
                        i = None
                        for c in range(NCH):
                            i = t.matmul(ps[:, 2560:2560 + T],
                                         ones_bf[:, 0:128], a2[:, c, :],
                                         start=(c == 0), stop=(c == NCH - 1),
                                         skip_group_check=True)
                        return i
                    t_sums3 = t_c.n

                if tt < NT:
                    for m in (14, 15):
                        @SEC(blk, "tensor")
                        def _(t, tt=tt, m=m):
                            return _wo(t, tt, m)
                        twm[tt * NCH + m] = t_c.n

                if t2 >= 0:
                    @SEC(blk, "vector")
                    def _(v, t_sums3=t_sums3):
                        return _stats(v, t_sums3)
                    v_rv3 = v_c.n

                if tt < NT:
                    @SEC(blk, "scalar")
                    def _(sc, tt=tt):
                        m2 = None
                        for m in range(2, NCH):
                            _wait(sc, t_c, twm[tt * NCH + m])
                            m2 = sc.copy(aoo[:, tt % 2, m, :],
                                         ps[:, (m % 2) * 512:
                                            (m % 2) * 512 + 512])
                            got = _inc(m2, c_c, 1)
                            assert got == ev[tt * NCH + m], (tt, m, got)
                        return None

                if t2 >= 0:
                    @SEC(blk, "scalar")
                    def _(sc, v_rv3=v_rv3):
                        _wait(sc, v_c, v_rv3)
                        return sc.activation(lnv[:, 0:T], lnv[:, 0:T], AF.Sqrt)
                    c_sqrt3 = c_c.n

                    @SEC(blk, "vector")
                    def _(v, c_sqrt3=c_sqrt3):
                        return _ab(v, c_sqrt3)

                    @SEC(blk, "vector")
                    def _(v, t2=t2, a2=a2):
                        v.tensor_tensor(a2, sqb[:], bc_ap(ab_bf[:, 0:T], NCH),
                                        ALU.mult)
                        return v.tensor_tensor(sqb[:], a2,
                                               bc_ap(ab_bf[:, 512:1024], NCH),
                                               ALU.add)
                    vlx_done[t2] = v_c.n

                    @blk.sync
                    def _(s, t2=t2):
                        _wait(s, v_c, vlx_done[t2])
                        _inc(s.dma_start(out=lx3[:, t2], in_=sqb[:]), st_lx, 16)
                    lx_st[t2] = st_lx.n

    G = 2
    NG = NIC // G
    with (
        nc.sbuf_tensor("c_xa", [128, NCH, T], F32) as xa2,
        nc.sbuf_tensor("c_ln3", [128, NCH, T], BF16) as ln3x,
        nc.sbuf_tensor("c_fuse", [128, NIC, T], BF16) as fuse,
        nc.sbuf_tensor("c_wgu", [128, 2, 2, G * D], BF16) as wgu_bf,
        nc.sbuf_tensor("c_wd", [128, 2, 4, D], BF16) as wd_bf,
        nc.sbuf_tensor("c_gb", [128, 4, T], BF16) as gb,
        nc.sbuf_tensor("c_ffo", [128, NCH, T], BF16) as ffo,
        nc.psum_tensor("c_ps", [128, 4096], F32) as ps,
    ):
        with nc.Block() as blk:
            block_guard(blk)
            sil_done = [None] * (NT * NG)
            upz_done = [None] * (NT * NG)
            dn_done = {}
            de_done = {}
            v3_done = [None] * NT
            st_out = [None] * NT
            g_wgu = [None] * (NT * NG)
            g_wd = {}
            gu_last = [None] * NT
            ln4_ab = [None] * NT

            for tt in range(NT):
                tc = slice(tt * T, (tt + 1) * T)

                @blk.sync
                def _(s, tt=tt, tc=tc):
                    _wait(s, st_x, st_xm[NT - 1] if tt == 0 else st_xm[tt])
                    if tt >= 1:
                        _wait(s, st_o, st_out[tt - 1])
                    _inc(s.dma_start(out=xa2[:], in_=xmid[:, tt]), s_xc, 16)
                xm_ld = s_xc.n

                if tt == 0:
                    @blk.sync
                    def _(s):
                        _wait(s, st_lx, lx_st[NT - 1])
                        _wait(s, st_x, st_xm[NT - 1])
                        _inc(s.dma_start(out=ln3x[:], in_=lx3[:, 0]), s_lx, 16)
                lx_ld = s_lx.n

                sil_jobs = []
                sil_base = c_c.n
                for grp in range(NG):
                    gi = tt * NG + grp
                    ic0 = grp * G
                    p2 = grp % 2

                    @blk.gpsimd
                    def _(g, tt=tt, grp=grp, gi=gi, ic0=ic0, p2=p2):
                        if gi == 0:
                            _wait(g, st_x, st_xm[NT - 1])
                            _wait(g, st_lx, lx_st[NT - 1])
                        if gi >= 2:
                            _wait(g, c_c, sil_done[gi - 2])
                            _wait(g, v_c, upz_done[gi - 2])
                        _inc(g.dma_start(out=wgu_bf[:, p2], in_=wgu_r[:, grp]),
                             g_g[p2], 16)
                    g_wgu[gi] = g_g[p2].n

                    @SEC(blk, "tensor")
                    def _(t, tt=tt, grp=grp, gi=gi, ic0=ic0, p2=p2):
                        _wait(t, g_g[p2], g_wgu[gi])
                        if grp <= 1:
                            _wait(t, s_lx, lx_ld)
                            if tt >= 1 and grp == 0:
                                _wait(t, v_b, de_done[(tt - 1, 1)])
                            if tt >= 1 and grp == 1:
                                _wait(t, v_c, ln4_ab[tt - 1])
                        if gi >= 2:
                            _wait(t, c_c, sil_done[gi - 2])
                            _wait(t, v_c, upz_done[gi - 2])
                        base = p2 * 2048
                        i = None
                        for j in range(G):
                            for c in range(NCH):
                                t.matmul(ps[:, base + j * 1024:base + j * 1024 + 512],
                                         wgu_bf[:, p2, 0, j * D + c * 128:
                                                j * D + (c + 1) * 128],
                                         ln3x[:, c, :],
                                         start=(c == 0), stop=(c == NCH - 1),
                                         skip_group_check=True)
                            for c in range(NCH):
                                i = t.matmul(ps[:, base + j * 1024 + 512:
                                                base + j * 1024 + 1024],
                                             wgu_bf[:, p2, 1, j * D + c * 128:
                                                   j * D + (c + 1) * 128],
                                             ln3x[:, c, :],
                                             start=(c == 0), stop=(c == NCH - 1),
                                             skip_group_check=True)
                        return i
                    t_gu = t_c.n
                    gu_last[tt] = t_gu

                    sil_jobs.append((grp, gi, ic0, p2, t_gu))
                    sil_done[gi] = sil_base + grp + 1

                    @SEC(blk, "vector")
                    def _(v, grp=grp, gi=gi, ic0=ic0, p2=p2, t_gu=t_gu):
                        _wait(v, t_c, t_gu)
                        _wait(v, c_c, sil_done[gi])
                        base = p2 * 2048
                        i = None
                        for j in range(G):
                            ic = ic0 + j
                            v.tensor_scalar_mul(fuse[:, ic, :],
                                                ps[:, base + j * 1024 + 512:
                                                   base + j * 1024 + 1024],
                                                su_sb[:, ic:ic + 1])
                            i = v.tensor_tensor(fuse[:, ic, :], gb[:, ic % 4],
                                                fuse[:, ic, :], ALU.mult)
                        return i
                    upz_done[gi] = v_c.n

                @SEC(blk, "scalar")
                def _(sc, sil_jobs=sil_jobs, tt=tt):
                    m = None
                    for grp, gi, ic0, p2, t_gu in sil_jobs:
                        _wait(sc, t_c, t_gu)
                        if gi == 0:
                            _wait(sc, s_sc, scales_ld)
                        base = p2 * 2048
                        for j in range(G):
                            ic = ic0 + j
                            m = sc.activation(gb[:, ic % 4],
                                              ps[:, base + j * 1024:
                                                 base + j * 1024 + 512],
                                              AF.Silu, scale=sg_sb[:, ic:ic + 1])
                        got = _inc(m, c_c, 1)
                        assert got == sil_done[tt * NG + grp], (tt, grp, got)
                    return None

                if tt + 1 < NT:
                    @SEC(blk, "scalar")
                    def _(sc, tt=tt):
                        _wait(sc, t_c, gu_last[tt])
                        _inc(sc.dma_start(out=ln3x[:], in_=lx3[:, tt + 1]),
                             s_lx, 16)
                        return None

                NGD = NIC // 4
                for half in range(2):
                    for gd in range(NGD):
                        ic0 = gd * 4
                        p2 = gd % 2

                        @blk.gpsimd
                        def _(g, tt=tt, half=half, gd=gd, ic0=ic0, p2=p2):
                            prev = None
                            if gd >= 2:
                                prev = (tt, half, gd - 2)
                            elif half == 1:
                                prev = (tt, 0, NGD - 2 + gd)
                            elif tt >= 1:
                                prev = (tt - 1, 1, NGD - 2 + gd)
                            if prev is not None:
                                _wait(g, t_b, dn_done[prev])
                            _inc(g.dma_start(
                                out=wd_bf[:, p2],
                                in_=wd_n[ic0 * 128:(ic0 + 4) * 128, :]
                                .rearrange("(a p) d -> p a d", p=128)),
                                 g_d[p2], 16)
                        g_wd[(tt, half, gd)] = g_d[p2].n

                        @SEC(blk, "tensor")
                        def _(t, tt=tt, half=half, gd=gd, ic0=ic0, p2=p2):
                            _wait(t, g_d[p2], g_wd[(tt, half, gd)])
                            if gd == 0:
                                _wait(t, v_c, upz_done[tt * NG + NG - 1])
                                if half == 0:
                                    _wait(t, c_c, sil_done[tt * NG + NG - 1])
                                else:
                                    _wait(t, v_b, de_done[(tt, 0)])
                            i = None
                            for j in range(4):
                                ic = ic0 + j
                                for m in range(8):
                                    mm = half * 8 + m
                                    i = t.matmul(ps[:, m * 512:(m + 1) * 512],
                                                 wd_bf[:, p2, j,
                                                       mm * 128:(mm + 1) * 128],
                                                 fuse[:, ic, :],
                                                 start=(ic == 0),
                                                 stop=(ic == NIC - 1),
                                                 skip_group_check=True)
                            _inc(i, t_b, 1)
                        dn_done[(tt, half, gd)] = t_b.n

                    @SEC(blk, "vector")
                    def _(v, tt=tt, half=half):
                        _wait(v, t_b, dn_done[(tt, half, NGD - 1)])
                        i = None
                        for m in range(8):
                            mm = half * 8 + m
                            i = v.tensor_scalar_mul(ffo[:, mm, :],
                                                    ps[:, m * 512:(m + 1) * 512],
                                                    sd_sb[:, mm:mm + 1])
                        _inc(i, v_b, 1)
                    de_done[(tt, half)] = v_b.n

                fv = fuse[:, 0:NCH, :]
                ln4_ab[tt] = emit_ln(blk, ps, ffo[:], fv, T)

                @SEC(blk, "vector")
                def _(v, tt=tt, fv=fv, xm_ld=xm_ld):
                    _wait(v, s_xc, xm_ld)
                    v.tensor_tensor(fv, ffo[:], bc_ap(ab_bf[:, 0:T], NCH), ALU.mult)
                    v.tensor_tensor(ffo[:], fv, bc_ap(ab_bf[:, 512:1024], NCH),
                                    ALU.add)
                    return v.tensor_tensor(xa2[:], xa2[:], ffo[:], ALU.add)
                v3_done[tt] = v_c.n

                @blk.sync
                def _(s, tt=tt, tc=tc):
                    _wait(s, v_c, v3_done[tt])
                    _inc(s.dma_start(out=outR[:, tt], in_=xa2[:]), st_o, 16)
                st_out[tt] = st_o.n

            @blk.sync
            def _(s):
                _wait(s, st_o, st_out[NT - 1])

    for sem, v in waits:
        assert v <= sem.n, f"wait {v} > final {sem.n} on {sem.h}"
    return nc


_CACHE = {}


def _prep_weights(wq, wk, wv, wo, wg, wu, wd, sg, su, sd):
    f32 = np.float32
    wq_r = np.ascontiguousarray(
        wq.T.reshape(NCH, 128, NH, 128).transpose(1, 2, 0, 3).reshape(128, -1),
        dtype=f32)
    wk_r = np.ascontiguousarray(
        wk.T.reshape(NCH, 128, 1024).transpose(1, 0, 2).reshape(128, -1), dtype=f32)
    wv_r = np.ascontiguousarray(
        wv.T.reshape(NCH, 128, 1024).transpose(1, 0, 2).reshape(128, -1), dtype=f32)
    wo_r = np.ascontiguousarray(
        wo.T.reshape(NCH, 128, D).transpose(1, 0, 2).reshape(128, -1), dtype=f32)
    wg_r = wg.reshape(NCH, 128, NIC, 128).transpose(1, 2, 0, 3).reshape(128, NIC // 2, 2 * D)
    wu_r = wu.reshape(NCH, 128, NIC, 128).transpose(1, 2, 0, 3).reshape(128, NIC // 2, 2 * D)
    wgu_r = np.ascontiguousarray(np.stack([wg_r, wu_r], axis=2))
    wd_n = np.ascontiguousarray(wd)
    sg_r = np.ascontiguousarray(sg.reshape(NIC, 128).T, dtype=f32)
    su_r = np.ascontiguousarray(su.reshape(NIC, 128).T, dtype=f32)
    sd_r = np.ascontiguousarray(sd.reshape(NCH, 128).T, dtype=f32)
    return dict(wq_r=wq_r, wk_r=wk_r, wv_r=wv_r, wo_r=wo_r, wgu_r=wgu_r,
                wd_n=wd_n, sg_r=sg_r, su_r=su_r, sd_r=sd_r)


def kernel(L0_attn_hidden_states, L0_attn_wq, L0_attn_wk, L0_attn_wv,
           L0_attn_wo, L0_ffn_w_gate, L0_ffn_w_up, L0_ffn_w_down,
           L0_ffn_s_gate, L0_ffn_s_up, L0_ffn_s_down):
    x = np.asarray(L0_attn_hidden_states, dtype=np.float32)
    B = x.shape[0]
    shared = _prep_weights(
        np.asarray(L0_attn_wq, dtype=np.float32),
        np.asarray(L0_attn_wk, dtype=np.float32),
        np.asarray(L0_attn_wv, dtype=np.float32),
        np.asarray(L0_attn_wo, dtype=np.float32),
        np.asarray(L0_ffn_w_gate, dtype=np.int8),
        np.asarray(L0_ffn_w_up, dtype=np.int8),
        np.asarray(L0_ffn_w_down, dtype=np.int8),
        np.asarray(L0_ffn_s_gate, dtype=np.float32),
        np.asarray(L0_ffn_s_up, dtype=np.float32),
        np.asarray(L0_ffn_s_down, dtype=np.float32))

    if "nc" not in _CACHE:
        _CACHE["nc"] = build_program()
    nc = _CACHE["nc"]

    in_maps = []
    for b in range(B):
        m = dict(shared)
        m["xR"] = np.ascontiguousarray(
            x[b].T.reshape(NCH, 128, NT, T).transpose(1, 2, 0, 3))
        in_maps.append(m)

    res = run_bass_kernel_spmd(nc, in_maps, core_ids=list(range(B)))
    out = np.stack([np.asarray(res.results[b]["outR"]).transpose(1, 3, 2, 0)
                    .reshape(S, D) for b in range(B)], axis=0)
    return np.ascontiguousarray(out, dtype=np.float32)


# version 15 (final)
# speedup vs baseline: 1.1039x; 1.0175x over previous
import numpy as np
import concourse.bass as bass
import concourse.mybir as mybir
from concourse.ap import AP
from concourse.bass_utils import run_bass_kernel_spmd

F32 = mybir.dt.float32
BF16 = mybir.dt.bfloat16
I8 = mybir.dt.int8
AF = mybir.ActivationFunctionType
ALU = mybir.AluOpType

D = 2048
S = 2048
NCH = 16
NH = 16
NKV = 8
INTER = 8192
NIC = 64
EPS = 1e-5
ISQ = float(np.float32(1.0) / np.sqrt(np.float32(128.0)))
T = 512
NT = S // T
TA = 512
NTA = S // TA


def bc_ap(ap2d, reps):
    return AP(tensor=ap2d.tensor, offset=ap2d.offset,
              ap=[list(ap2d.ap[0]), [0, reps], list(ap2d.ap[1])])


class Sem:
    __slots__ = ("h", "n")

    def __init__(self, stack, nc, name):
        self.h = stack.enter_context(nc.semaphore(name))
        self.n = 0


def build_program():
    from contextlib import ExitStack
    nc = bass.Bass("TRN2", target_bir_lowering=False,
                   detect_race_conditions=False)
    waits = []

    xR = nc.dram_tensor("xR", [128, NT, NCH, T], F32, kind="ExternalInput").ap()
    wq_r = nc.dram_tensor("wq_r", [128, NH * D], F32, kind="ExternalInput").ap()
    wk_r = nc.dram_tensor("wk_r", [128, NCH * 1024], F32, kind="ExternalInput").ap()
    wv_r = nc.dram_tensor("wv_r", [128, NCH * 1024], F32, kind="ExternalInput").ap()
    wo_r = nc.dram_tensor("wo_r", [128, NCH * D], F32, kind="ExternalInput").ap()
    wgu_r = nc.dram_tensor("wgu_r", [128, NIC // 2, 2, 2 * D], I8,
                           kind="ExternalInput").ap()
    wd_n = nc.dram_tensor("wd_n", [INTER, D], I8, kind="ExternalInput").ap()
    sg_r = nc.dram_tensor("sg_r", [128, NIC], F32, kind="ExternalInput").ap()
    su_r = nc.dram_tensor("su_r", [128, NIC], F32, kind="ExternalInput").ap()
    sd_r = nc.dram_tensor("sd_r", [128, NCH], F32, kind="ExternalInput").ap()
    import os
    _dbg = os.environ.get("K_DBG_SCRATCH") == "1"
    outR = nc.dram_tensor("outR", [128, NT, NCH, T], F32, kind="ExternalOutput").ap()
    attno = nc.dram_tensor("attno", [128, NT, NCH, T], BF16,
                           kind="ExternalOutput" if _dbg else "Internal").ap()
    xmid = nc.dram_tensor("xmid", [128, NT, NCH, T], F32,
                          kind="ExternalOutput" if _dbg else "Internal").ap()
    lx3 = nc.dram_tensor("lx3", [128, NT, NCH, T], BF16, kind="Internal").ap()


    stack = nc.ctx

    s_sc = Sem(stack, nc, "s_sc")
    s_x = [Sem(stack, nc, "s_x0"), Sem(stack, nc, "s_x1")]
    s_xb = Sem(stack, nc, "s_xb")
    s_xc = Sem(stack, nc, "s_xc")
    st_a = [Sem(stack, nc, "st_a0"), Sem(stack, nc, "st_a1")]
    st_x = Sem(stack, nc, "st_x")
    st_o = Sem(stack, nc, "st_o")
    st_lx = Sem(stack, nc, "st_lx")
    s_lx = Sem(stack, nc, "s_lx")
    g_wk = Sem(stack, nc, "g_wk")
    g_wv = Sem(stack, nc, "g_wv")
    g_wo = Sem(stack, nc, "g_wo")
    g_q = [Sem(stack, nc, "g_q0"), Sem(stack, nc, "g_q1")]
    g_g = [Sem(stack, nc, "g_g0"), Sem(stack, nc, "g_g1")]
    g_u = [Sem(stack, nc, "g_u0"), Sem(stack, nc, "g_u1")]
    g_d = [Sem(stack, nc, "g_d0"), Sem(stack, nc, "g_d1")]
    t_c = Sem(stack, nc, "t_c")
    v_c = Sem(stack, nc, "v_c")
    c_c = Sem(stack, nc, "c_c")
    t_b = Sem(stack, nc, "t_b")
    v_b = Sem(stack, nc, "v_b")
    c_b = Sem(stack, nc, "c_b")
    COMPUTE_SEMS = (t_c, v_c, c_c, t_b, v_b, c_b)
    ENG_SEMS = {"tensor": (t_c, t_b), "vector": (v_c, v_b), "scalar": (c_c, c_b)}

    def _inc(inst, sem, k=1):
        inst.then_inc(sem.h, k)
        sem.n += k
        return sem.n

    def _wait(eng, sem, v):
        if v <= 0:
            return
        waits.append((sem, v))
        eng.wait_ge(sem.h, v)

    def SEC(blk, name):
        def deco(body):
            dec = getattr(blk, name)

            @dec
            def _(eng):
                pr = ENG_SEMS.get(name)
                if pr:
                    _wait(eng, pr[0], pr[0].n)
                    _wait(eng, pr[1], pr[1].n)
                last = body(eng)
                if pr and last is not None:
                    _inc(last, pr[0], 1)
            return None
        return deco

    def block_guard(blk):
        for name in ("tensor", "vector", "scalar"):
            dec = getattr(blk, name)
            snap = [(s, s.n) for s in COMPUTE_SEMS]

            @dec
            def _(eng, snap=snap):
                for s, v in snap:
                    _wait(eng, s, v)

    ones_bf = stack.enter_context(nc.sbuf_tensor("ones_bf", [128, 128], BF16))
    ones_f = stack.enter_context(nc.sbuf_tensor("ones_f", [128, 128], F32))
    sg_sb = stack.enter_context(nc.sbuf_tensor("sg_sb", [128, NIC], F32))
    su_sb = stack.enter_context(nc.sbuf_tensor("su_sb", [128, NIC], F32))
    sd_sb = stack.enter_context(nc.sbuf_tensor("sd_sb", [128, NCH], F32))
    lnm = stack.enter_context(nc.sbuf_tensor("lnm", [128, 512], F32))
    lnv = stack.enter_context(nc.sbuf_tensor("lnv", [128, 512], F32))
    lnt = stack.enter_context(nc.sbuf_tensor("lnt", [128, 512], F32))
    ab_bf = stack.enter_context(nc.sbuf_tensor("ab_bf", [128, 1024], BF16))


    res_scope = ExitStack()
    ln1x = res_scope.enter_context(nc.sbuf_tensor("ln1x", [128, NCH, S], BF16))
    K_sb = res_scope.enter_context(nc.sbuf_tensor("K_sb", [128, NKV, S], BF16))
    V_sb = res_scope.enter_context(nc.sbuf_tensor("V_sb", [128, NCH, 1024], BF16))

    a_scope = ExitStack()
    wk_bf = a_scope.enter_context(nc.sbuf_tensor("a2_wk", [128, NCH, 1024], BF16))
    wv_bf = a_scope.enter_context(nc.sbuf_tensor("a2_wv", [128, NCH, 1024], BF16))

    with nc.Block() as blk:
        @SEC(blk, "vector")
        def _(v):
            v.memset(ones_bf[:], 1.0)
            return v.memset(ones_f[:], 1.0)

        @blk.sync
        def _(s):
            _inc(s.dma_start(out=sg_sb[:], in_=sg_r[:]), s_sc, 16)
            _wait(s, s_sc, 16)
            _inc(s.dma_start(out=su_sb[:], in_=su_r[:]), s_sc, 16)
            _wait(s, s_sc, 32)
            _inc(s.dma_start(out=sd_sb[:], in_=sd_r[:]), s_sc, 16)
    scales_ld = s_sc.n

    def emit_ln(blk, ps, in_bf, sq_out, Tt, pre_sq=None):
        @SEC(blk, "vector")
        def _(v):
            if pre_sq is not None:
                pre_sq(v)
            return v.tensor_tensor(sq_out, in_bf, in_bf, ALU.mult)
        v_sq = v_c.n

        @SEC(blk, "tensor")
        def _(t):
            _wait(t, v_c, v_sq)
            for c in range(NCH):
                t.matmul(ps[:, 2048:2048 + Tt], ones_bf[:, 0:128],
                         in_bf[:, c, :], start=(c == 0), stop=(c == NCH - 1),
                         skip_group_check=True)
            i = None
            for c in range(NCH):
                i = t.matmul(ps[:, 2560:2560 + Tt], ones_bf[:, 0:128],
                             sq_out[:, c, :], start=(c == 0), stop=(c == NCH - 1),
                             skip_group_check=True)
            return i
        t_sums = t_c.n

        @SEC(blk, "vector")
        def _(v):
            _wait(v, t_c, t_sums)
            v.tensor_scalar_mul(lnm[:, 0:Tt], ps[:, 2048:2048 + Tt], 1.0 / D)
            v.tensor_scalar_mul(lnv[:, 0:Tt], ps[:, 2560:2560 + Tt], 1.0 / D)
            v.tensor_tensor(lnt[:, 0:Tt], lnm[:, 0:Tt], lnm[:, 0:Tt], ALU.mult)
            v.tensor_tensor(lnv[:, 0:Tt], lnv[:, 0:Tt], lnt[:, 0:Tt], ALU.subtract)
            v.tensor_scalar_add(lnv[:, 0:Tt], lnv[:, 0:Tt], EPS)
            return v.reciprocal(lnv[:, 0:Tt], lnv[:, 0:Tt])
        v_rv = v_c.n

        @SEC(blk, "scalar")
        def _(sc):
            _wait(sc, v_c, v_rv)
            return sc.activation(lnv[:, 0:Tt], lnv[:, 0:Tt], AF.Sqrt)
        c_sqrt = c_c.n

        @SEC(blk, "vector")
        def _(v):
            _wait(v, c_c, c_sqrt)
            v.tensor_copy(ab_bf[:, 0:Tt], lnv[:, 0:Tt])
            return v.scalar_tensor_tensor(ab_bf[:, 512:512 + Tt], lnm[:, 0:Tt],
                                          -1.0, lnv[:, 0:Tt], ALU.mult, ALU.mult)
        return v_c.n

    with nc.psum_tensor("a_ps", [128, 4096], F32) as ps:
        with nc.Block() as blk:
            block_guard(blk)
            x_ld = [None] * NTA
            v_sq = [None] * NTA
            v_nm = [None] * NTA

            @blk.gpsimd
            def _(g):
                wkr = wk_r.rearrange("p (c k) -> p c k", c=NCH)
                wvr = wv_r.rearrange("p (c k) -> p c k", c=NCH)
                x_ld[0] = _inc(g.dma_start(out=ln1x[:, :, 0:TA],
                                           in_=xR[:, 0]), g_g[0], 16)
                x_ld[1] = _inc(g.dma_start(out=ln1x[:, :, TA:2 * TA],
                                           in_=xR[:, 1]), g_g[1], 16)
                _inc(g.dma_start(out=wk_bf[:, :, 0:512],
                                 in_=wkr[:, :, 0:512]), g_wk, 16)
                _wait(g, g_g[0], 16)
                x_ld[2] = _inc(g.dma_start(out=ln1x[:, :, 2 * TA:3 * TA],
                                           in_=xR[:, 2]), g_g[0], 16)
                _wait(g, g_wk, 16)
                _inc(g.dma_start(out=wk_bf[:, :, 512:1024],
                                 in_=wkr[:, :, 512:1024]), g_wk, 16)
                _wait(g, g_g[1], 16)
                x_ld[3] = _inc(g.dma_start(out=ln1x[:, :, 3 * TA:4 * TA],
                                           in_=xR[:, 3]), g_g[1], 16)
                _inc(g.dma_start(out=wv_bf[:, 0:8, :],
                                 in_=wvr[:, 0:8, :]), g_wv, 16)
                _wait(g, g_wv, 16)
                _inc(g.dma_start(out=wv_bf[:, 8:16, :],
                                 in_=wvr[:, 8:16, :]), g_wv, 16)

            def emit_ln1(ti):
                t0 = ti * TA
                xln = ln1x[:, :, t0:t0 + TA]
                sqK = K_sb[:, :, t0:t0 + TA]
                sqV = V_sb[:, 4 * ti:4 * ti + 4, :].rearrange(
                    "p a (b t) -> p (a b) t", b=2)

                @SEC(blk, "vector")
                def _(v, ti=ti, xln=xln, sqK=sqK, sqV=sqV):
                    _wait(v, g_g[ti % 2], x_ld[ti])
                    v.tensor_tensor(sqK, xln[:, 0:8, :], xln[:, 0:8, :],
                                    ALU.mult)
                    return v.tensor_tensor(sqV, xln[:, 8:16, :],
                                           xln[:, 8:16, :], ALU.mult)
                v_sq[ti] = v_c.n

                @SEC(blk, "tensor")
                def _(t, ti=ti, xln=xln, sqK=sqK, sqV=sqV):
                    _wait(t, v_c, v_sq[ti])
                    for c in range(NCH):
                        t.matmul(ps[:, 2048:2048 + TA], ones_bf[:, 0:128],
                                 xln[:, c, :], start=(c == 0),
                                 stop=(c == NCH - 1), skip_group_check=True)
                    i = None
                    for c in range(NCH):
                        src = sqK[:, c, :] if c < 8 else sqV[:, c - 8, :]
                        i = t.matmul(ps[:, 2560:2560 + TA], ones_bf[:, 0:128],
                                     src, start=(c == 0), stop=(c == NCH - 1),
                                     skip_group_check=True)
                    return i
                t_sums = t_c.n

                @SEC(blk, "vector")
                def _(v, t_sums=t_sums):
                    _wait(v, t_c, t_sums)
                    v.tensor_scalar_mul(lnm[:, 0:TA], ps[:, 2048:2048 + TA],
                                        1.0 / D)
                    v.tensor_scalar_mul(lnv[:, 0:TA], ps[:, 2560:2560 + TA],
                                        1.0 / D)
                    v.tensor_tensor(lnt[:, 0:TA], lnm[:, 0:TA], lnm[:, 0:TA],
                                    ALU.mult)
                    v.tensor_tensor(lnv[:, 0:TA], lnv[:, 0:TA], lnt[:, 0:TA],
                                    ALU.subtract)
                    v.tensor_scalar_add(lnv[:, 0:TA], lnv[:, 0:TA], EPS)
                    return v.reciprocal(lnv[:, 0:TA], lnv[:, 0:TA])
                v_rv = v_c.n

                @SEC(blk, "scalar")
                def _(sc, v_rv=v_rv):
                    _wait(sc, v_c, v_rv)
                    return sc.activation(lnv[:, 0:TA], lnv[:, 0:TA], AF.Sqrt)
                c_sqrt = c_c.n

                @SEC(blk, "vector")
                def _(v, c_sqrt=c_sqrt, xln=xln):
                    _wait(v, c_c, c_sqrt)
                    v.tensor_copy(ab_bf[:, 0:TA], lnv[:, 0:TA])
                    v.scalar_tensor_tensor(ab_bf[:, 512:512 + TA],
                                           lnm[:, 0:TA], -1.0, lnv[:, 0:TA],
                                           ALU.mult, ALU.mult)
                    v.tensor_tensor(xln, xln, bc_ap(ab_bf[:, 0:TA], NCH),
                                    ALU.mult)
                    return v.tensor_tensor(xln, xln,
                                           bc_ap(ab_bf[:, 512:512 + TA], NCH),
                                           ALU.add)
                v_nm[ti] = v_c.n

            ev = {}

            def emit_k(tj, ckvs):
                for ckv in ckvs:
                    kit = tj * NKV + ckv

                    @SEC(blk, "tensor")
                    def _(t, tj=tj, ckv=ckv, kit=kit):
                        if ckv == 0:
                            _wait(t, v_c, v_nm[tj])
                        if kit == 0:
                            _wait(t, g_wk, 16)
                        if tj == 0 and ckv == 4:
                            _wait(t, g_wk, 32)
                        if kit >= 2:
                            _wait(t, v_c, ev[kit - 2])
                        pcol = (kit % 2) * 512
                        i = None
                        for c in range(NCH):
                            i = t.matmul(ps[:, pcol:pcol + 512],
                                         wk_bf[:, c, ckv * 128:(ckv + 1) * 128],
                                         ln1x[:, c, tj * T:(tj + 1) * T],
                                         start=(c == 0), stop=(c == NCH - 1),
                                         skip_group_check=True)
                        return i
                    tk = t_c.n

                    @SEC(blk, "vector")
                    def _(v, tj=tj, ckv=ckv, kit=kit, tk=tk):
                        _wait(v, t_c, tk)
                        pcol = (kit % 2) * 512
                        return v.tensor_copy(K_sb[:, ckv, tj * T:(tj + 1) * T],
                                             ps[:, pcol:pcol + 512])
                    ev[kit] = v_c.n

            emit_ln1(0)
            emit_ln1(1)
            emit_k(0, range(0, 4))
            emit_ln1(2)
            emit_k(0, range(4, 8))
            emit_ln1(3)
            for tj in range(1, NT):
                emit_k(tj, range(NKV))

            evv = [None] * NCH
            for ck in range(NCH):
                @SEC(blk, "tensor")
                def _(t, ck=ck):
                    if ck == 0:
                        _wait(t, g_wv, 16)
                        _wait(t, v_c, ev[NKV * NT - 1])
                    if ck >= 2:
                        _wait(t, c_c, evv[ck - 2])
                    pcol = (ck % 2) * 1024
                    i = None
                    for c in range(NCH):
                        if ck == 0 and c == 8:
                            _wait(t, g_wv, 32)
                        for nh in range(2):
                            i = t.matmul(
                                ps[:, pcol + nh * 512:pcol + nh * 512 + 512],
                                ln1x[:, c, ck * 128:(ck + 1) * 128],
                                wv_bf[:, c, nh * 512:(nh + 1) * 512],
                                start=(c == 0), stop=(c == NCH - 1),
                                skip_group_check=True)
                    return i
                tv = t_c.n

                @SEC(blk, "scalar")
                def _(sc, ck=ck, tv=tv):
                    _wait(sc, t_c, tv)
                    pcol = (ck % 2) * 1024
                    return sc.copy(V_sb[:, ck, :], ps[:, pcol:pcol + 1024])
                evv[ck] = c_c.n

    a_scope.close()

    with (
        nc.sbuf_tensor("b1_wq", [128, 2, D], BF16) as wq_bf,
        nc.sbuf_tensor("b1_qbf", [128, 2, T], BF16) as qbf,
        nc.sbuf_tensor("b1_probs", [128, 2, NCH * T], BF16) as probs,
        nc.sbuf_tensor("b1_obf", [128, 2, T], BF16) as obf,
        nc.sbuf_tensor("b1_osc", [128, 2, T], BF16) as osc,
        nc.sbuf_tensor("b1_rec", [128, T], F32) as rec,
        nc.sbuf_tensor("b1_tree", [128, 2, 8, T], BF16) as tree,
        nc.psum_tensor("b1_ps", [128, 4096], F32) as ps,
    ):
        PO, PQS = 3072, 3584
        with nc.Block() as blk:
            block_guard(blk)

            n_it = NH * NT
            wq_ld = [None] * NH
            tq_done = [None] * (n_it + 1)
            vq_done = [None] * (n_it + 1)
            sc_exp = [None] * (n_it * 8)
            tpv_done = [None] * n_it
            sums_done = [None] * n_it
            tree_done = [None] * n_it
            rc_done = [None] * n_it
            obf_done = [None] * n_it
            osc_done = [None] * n_it
            st_ao = [None] * n_it

            @blk.gpsimd
            def _(g):
                _inc(g.dma_start(out=wq_bf[:, 0], in_=wq_r[:, 0:D]), g_q[0], 16)
            wq_ld[0] = g_q[0].n

            def emit_q(t, j):
                h, tt = j // NT, j % NT
                if tt == 0:
                    _wait(t, g_q[h % 2], wq_ld[h])
                if j >= 3:
                    _wait(t, v_b, rc_done[j - 3])
                _wait(t, v_c, vq_done[j - 1] if j >= 1 else 0)
                i = None
                for c in range(NCH):
                    i = t.matmul(ps[:, PQS:PQS + 512],
                                 wq_bf[:, h % 2, c * 128:(c + 1) * 128],
                                 ln1x[:, c, tt * T:(tt + 1) * T],
                                 start=(c == 0), stop=(c == NCH - 1),
                                 skip_group_check=True)
                return i

            @SEC(blk, "tensor")
            def _(t):
                return emit_q(t, 0)
            tq_done[0] = t_c.n

            @SEC(blk, "vector")
            def _(v):
                _wait(v, t_c, tq_done[0])
                return v.tensor_copy(qbf[:, 0], ps[:, PQS:PQS + 512])
            vq_done[0] = v_c.n

            C_base = c_c.n
            exp_jobs = []
            for g2 in range(n_it * 8):
                sc_exp[g2] = C_base + g2 + 1

            for i in range(n_it):
                h, tt = i // NT, i % NT
                kvh = h // 2

                if tt == 0 and h + 1 < NH:
                    @blk.gpsimd
                    def _(g, h=h):
                        _wait(g, t_c, tq_done[h * NT - 1] if h >= 1 else 0)
                        _inc(g.dma_start(out=wq_bf[:, (h + 1) % 2],
                                         in_=wq_r[:, (h + 1) * D:(h + 2) * D]),
                             g_q[(h + 1) % 2], 16)
                    wq_ld[h + 1] = g_q[(h + 1) % 2].n

                c0 = C_base + 8 * i

                @SEC(blk, "tensor")
                def _(t, i=i, kvh=kvh, c0=c0):
                    _wait(t, v_c, vq_done[i])
                    for g in range(8):
                        _wait(t, c_c, c0 + g - 2)
                        base = (g % 3) * 1024
                        for j in range(2):
                            kc = 2 * g + j
                            m = t.matmul(
                                ps[:, base + j * 512:base + j * 512 + 512],
                                K_sb[:, kvh, kc * 128:(kc + 1) * 128],
                                qbf[:, i % 2],
                                start=True, stop=True, skip_group_check=True)
                        _inc(m, t_c, 1)
                t_sc_base = t_c.n - 8

                if i + 1 < n_it:
                    @SEC(blk, "tensor")
                    def _(t, i=i):
                        return emit_q(t, i + 1)
                    tq_done[i + 1] = t_c.n

                exp_jobs.append((i, t_sc_base))

                if i + 1 < n_it:
                    @SEC(blk, "vector")
                    def _(v, i=i):
                        _wait(v, t_c, tq_done[i + 1])
                        return v.tensor_copy(qbf[:, (i + 1) % 2],
                                             ps[:, PQS:PQS + 512])
                    vq_done[i + 1] = v_c.n

                @SEC(blk, "tensor")
                def _(t, i=i, kvh=kvh):
                    if i >= 1:
                        _wait(t, v_c, vq_done[min(i + 1, n_it - 1)])
                        _wait(t, v_c, tree_done[i - 1])
                        m = t.matmul(ps[:, PQS:PQS + 512], ones_bf[:, 0:128],
                                     tree[:, (i - 1) % 2, 0, :],
                                     start=True, stop=True, skip_group_check=True)
                        sums_done[i - 1] = _inc(m, t_b, 1)
                    if i >= 1:
                        _wait(t, v_b, obf_done[i - 1])
                    for g in range(8):
                        _wait(t, c_c, sc_exp[i * 8 + g])
                        for j in range(2):
                            kc = 2 * g + j
                            m = t.matmul(ps[:, PO:PO + 512],
                                         V_sb[:, kc, kvh * 128:(kvh + 1) * 128],
                                         probs[:, i % 2, kc * T:(kc + 1) * T],
                                         start=(kc == 0), stop=(kc == NCH - 1),
                                         skip_group_check=True)
                    tpv_done[i] = _inc(m, t_b, 1)

                @SEC(blk, "vector")
                def _(v, i=i):
                    _wait(v, c_c, sc_exp[i * 8 + 7])
                    if i >= 2:
                        _wait(v, t_b, sums_done[i - 2])
                    tr = tree[:, i % 2]
                    v.tensor_tensor(tr[:, :, :],
                                    probs[:, i % 2, 0:8 * T].rearrange(
                                        "p (a t) -> p a t", a=8),
                                    probs[:, i % 2, 8 * T:16 * T].rearrange(
                                        "p (a t) -> p a t", a=8), ALU.add)
                    v.tensor_tensor(tr[:, 0:4, :], tr[:, 0:4, :], tr[:, 4:8, :],
                                    ALU.add)
                    v.tensor_tensor(tr[:, 0:2, :], tr[:, 0:2, :], tr[:, 2:4, :],
                                    ALU.add)
                    return v.tensor_tensor(tr[:, 0:1, :], tr[:, 0:1, :],
                                           tr[:, 1:2, :], ALU.add)
                tree_done[i] = v_c.n

                @SEC(blk, "vector")
                def _(v, i=i):
                    if i >= 2:
                        if i >= 4:
                            _wait(v, st_a[i % 2], st_ao[i - 4])
                        m = v.tensor_tensor(osc[:, i % 2], obf[:, i % 2],
                                            rec[:], ALU.mult)
                        osc_done[i - 2] = _inc(m, v_b, 1)
                    _wait(v, t_b, tpv_done[i])
                    m = v.tensor_copy(obf[:, i % 2], ps[:, PO:PO + 512])
                    obf_done[i] = _inc(m, v_b, 1)
                    if i >= 1:
                        _wait(v, t_b, sums_done[i - 1])
                        m = v.reciprocal(rec[:], ps[:, PQS:PQS + 512])
                        rc_done[i - 1] = _inc(m, v_b, 1)
                    return None

                if i >= 2:
                    @blk.sync
                    def _(s, i=i):
                        j = i - 2
                        hh, ttt = j // NT, j % NT
                        _wait(s, v_b, osc_done[j])
                        _inc(s.dma_start(out=attno[:, ttt, hh, :],
                                         in_=osc[:, j % 2]), st_a[j % 2], 16)
                    st_ao[i - 2] = st_a[i % 2].n

            @SEC(blk, "scalar")
            def _(sc):
                for i2, tsb in exp_jobs:
                    if i2 >= 2:
                        _wait(sc, t_b, tpv_done[i2 - 2])
                    for g in range(8):
                        _wait(sc, t_c, tsb + g + 1)
                        base = (g % 3) * 1024
                        m = sc.activation(
                            probs[:, i2 % 2, 2 * g * T:(2 * g + 2) * T],
                            ps[:, base:base + 1024], AF.Exp, scale=ISQ)
                        got = _inc(m, c_c, 1)
                        assert got == sc_exp[i2 * 8 + g], (i2, g, got)
                return None

            for i in (n_it, n_it + 1):
                @SEC(blk, "tensor")
                def _(t, i=i):
                    if i - 1 < n_it:
                        _wait(t, v_b, rc_done[i - 2])
                        _wait(t, v_c, tree_done[i - 1])
                        m = t.matmul(ps[:, PQS:PQS + 512], ones_bf[:, 0:128],
                                     tree[:, (i - 1) % 2, 0, :],
                                     start=True, stop=True, skip_group_check=True)
                        sums_done[i - 1] = _inc(m, t_b, 1)
                        return None
                    return None

                @SEC(blk, "vector")
                def _(v, i=i):
                    _wait(v, st_a[i % 2], st_ao[i - 4])
                    m = v.tensor_tensor(osc[:, i % 2], obf[:, (i - 2) % 2],
                                        rec[:], ALU.mult)
                    osc_done[i - 2] = _inc(m, v_b, 1)
                    if i - 1 < n_it:
                        _wait(v, t_b, sums_done[i - 1])
                        m = v.reciprocal(rec[:], ps[:, PQS:PQS + 512])
                        rc_done[i - 1] = _inc(m, v_b, 1)
                    return None

                @blk.sync
                def _(s, i=i):
                    j = i - 2
                    hh, ttt = j // NT, j % NT
                    _wait(s, v_b, osc_done[j])
                    _inc(s.dma_start(out=attno[:, ttt, hh, :],
                                     in_=osc[:, j % 2]), st_a[j % 2], 16)
                st_ao[i - 2] = st_a[i % 2].n

            b1_store_total = (st_a[0].n, st_a[1].n)

    res_scope.close()

    st_xm = [None] * NT
    lx_st = [None] * NT
    with (
        nc.sbuf_tensor("b2_wo", [128, NCH, D], BF16) as wo_bf,
        nc.sbuf_tensor("b2_ao", [128, 2, NCH, T], BF16) as ao,
        nc.sbuf_tensor("b2_xa", [128, NCH, T], F32) as xa,
        nc.sbuf_tensor("b2_aoo", [128, 2, NCH, T], BF16) as aoo,
        nc.sbuf_tensor("b2_sq", [128, NCH, T], BF16) as sqb,
        nc.psum_tensor("b2_ps", [128, 4096], F32) as ps,
    ):
        with nc.Block() as blk:
            block_guard(blk)

            @blk.gpsimd
            def _(g):
                _wait(g, st_a[0], b1_store_total[0])
                _wait(g, st_a[1], b1_store_total[1])
                wor = wo_r.rearrange("p (c o) -> p c o", c=NCH)
                _inc(g.dma_start(out=wo_bf[:, :, 0:1024],
                                 in_=wor[:, :, 0:1024]), g_wo, 16)
                _inc(g.dma_start(out=wo_bf[:, :, 1024:2048],
                                 in_=wor[:, :, 1024:2048]), g_wk, 16)
            wo_rdy = g_wo.n
            wo_rdy2 = g_wk.n

            ao_ld = []

            @blk.sync
            def _(s):
                _wait(s, st_a[0], b1_store_total[0])
                _wait(s, st_a[1], b1_store_total[1])
                _inc(s.dma_start(out=ao[:, 0], in_=attno[:, 0]), s_x[0], 16)
                ao_ld.append(s_x[0].n)

            ev = [None] * (NT * NCH)
            twm = {}
            v3_done = [None] * NT
            vlx_done = [None] * NT
            xa_ld = [None] * NT
            for tt in range(NT + 1):
                t2 = tt - 1

                if tt < NT and tt + 1 < NT:
                    @blk.sync
                    def _(s, tt=tt):
                        if tt >= 1:
                            _wait(s, c_c, ev[(tt - 1) * NCH + NCH - 1])
                        _inc(s.dma_start(out=ao[:, (tt + 1) % 2],
                                         in_=attno[:, tt + 1]),
                             s_x[(tt + 1) % 2], 16)
                        ao_ld.append(s_x[(tt + 1) % 2].n)

                if t2 >= 0:
                    @blk.sync
                    def _(s, t2=t2):
                        if t2 >= 1:
                            _wait(s, v_c, vlx_done[t2 - 1])
                            _wait(s, st_x, st_xm[t2 - 1])
                        _inc(s.dma_start(out=xa[:], in_=xR[:, t2]), s_xb, 16)
                    xa_ld[t2] = s_xb.n

                if tt < NT:
                    e0 = c_c.n
                    for m in range(NCH):
                        ev[tt * NCH + m] = e0 + m + 1 + \
                            (1 if (t2 >= 0 and m >= 2) else 0)

                a2 = aoo[:, t2 % 2] if t2 >= 0 else None

                if t2 >= 0:
                    @SEC(blk, "vector")
                    def _(v, t2=t2, a2=a2):
                        _wait(v, c_c, ev[t2 * NCH + NCH - 1])
                        if t2 >= 1:
                            _wait(v, st_lx, lx_st[t2 - 1])
                        return v.tensor_tensor(sqb[:], a2, a2, ALU.mult)
                    v_sq2 = v_c.n

                def _wo(t, tt, m):
                    if m == 0:
                        _wait(t, g_wo, wo_rdy)
                        _wait(t, s_x[tt % 2], ao_ld[tt])
                    if m == 8 and tt == 0:
                        _wait(t, g_wk, wo_rdy2)
                    mi = tt * NCH + m
                    if mi >= 2:
                        _wait(t, c_c, ev[mi - 2])
                    pcol = (m % 2) * 512
                    i = None
                    for c in range(NCH):
                        i = t.matmul(ps[:, pcol:pcol + 512],
                                     wo_bf[:, c, m * 128:(m + 1) * 128],
                                     ao[:, tt % 2, c, :],
                                     start=(c == 0), stop=(c == NCH - 1),
                                     skip_group_check=True)
                    return i

                def _stats(v, twait):
                    _wait(v, t_c, twait)
                    v.tensor_scalar_mul(lnm[:, 0:T], ps[:, 2048:2048 + T],
                                        1.0 / D)
                    v.tensor_scalar_mul(lnv[:, 0:T], ps[:, 2560:2560 + T],
                                        1.0 / D)
                    v.tensor_tensor(lnt[:, 0:T], lnm[:, 0:T], lnm[:, 0:T],
                                    ALU.mult)
                    v.tensor_tensor(lnv[:, 0:T], lnv[:, 0:T], lnt[:, 0:T],
                                    ALU.subtract)
                    v.tensor_scalar_add(lnv[:, 0:T], lnv[:, 0:T], EPS)
                    return v.reciprocal(lnv[:, 0:T], lnv[:, 0:T])

                def _ab(v, cwait):
                    _wait(v, c_c, cwait)
                    v.tensor_copy(ab_bf[:, 0:T], lnv[:, 0:T])
                    return v.scalar_tensor_tensor(ab_bf[:, 512:512 + T],
                                                  lnm[:, 0:T], -1.0,
                                                  lnv[:, 0:T], ALU.mult,
                                                  ALU.mult)

                if tt < NT:
                    for m in (0, 1):
                        @SEC(blk, "tensor")
                        def _(t, tt=tt, m=m):
                            return _wo(t, tt, m)
                        twm[tt * NCH + m] = t_c.n

                if t2 >= 0:
                    @SEC(blk, "tensor")
                    def _(t, v_sq2=v_sq2, a2=a2):
                        _wait(t, v_c, v_sq2)
                        for c in range(NCH):
                            t.matmul(ps[:, 2048:2048 + T], ones_bf[:, 0:128],
                                     a2[:, c, :], start=(c == 0),
                                     stop=(c == NCH - 1),
                                     skip_group_check=True)
                        i = None
                        for c in range(NCH):
                            i = t.matmul(ps[:, 2560:2560 + T],
                                         ones_bf[:, 0:128], sqb[:, c, :],
                                         start=(c == 0), stop=(c == NCH - 1),
                                         skip_group_check=True)
                        return i
                    t_sums2 = t_c.n

                    @SEC(blk, "vector")
                    def _(v, t_sums2=t_sums2):
                        return _stats(v, t_sums2)
                    v_rv2 = v_c.n

                if tt < NT:
                    @SEC(blk, "scalar")
                    def _(sc, tt=tt):
                        m2 = None
                        for m in (0, 1):
                            _wait(sc, t_c, twm[tt * NCH + m])
                            if m == 0 and tt >= 2:
                                _wait(sc, v_c, vlx_done[tt - 2])
                            m2 = sc.copy(aoo[:, tt % 2, m, :],
                                         ps[:, (m % 2) * 512:
                                            (m % 2) * 512 + 512])
                            got = _inc(m2, c_c, 1)
                            assert got == ev[tt * NCH + m], (tt, m, got)
                        return None

                if t2 >= 0:
                    @SEC(blk, "scalar")
                    def _(sc, v_rv2=v_rv2):
                        _wait(sc, v_c, v_rv2)
                        return sc.activation(lnv[:, 0:T], lnv[:, 0:T], AF.Sqrt)
                    c_sqrt2 = c_c.n
                    if tt < NT:
                        assert c_sqrt2 == e0 + 3, (tt, c_sqrt2, e0)

                    @SEC(blk, "vector")
                    def _(v, c_sqrt2=c_sqrt2):
                        return _ab(v, c_sqrt2)

                    @SEC(blk, "vector")
                    def _(v, t2=t2, a2=a2):
                        _wait(v, s_xb, xa_ld[t2])
                        v.tensor_tensor(sqb[:], a2, bc_ap(ab_bf[:, 0:T], NCH),
                                        ALU.mult)
                        v.tensor_tensor(a2, sqb[:],
                                        bc_ap(ab_bf[:, 512:1024], NCH),
                                        ALU.add)
                        return v.tensor_tensor(xa[:], xa[:], a2, ALU.add)
                    v3_done[t2] = v_c.n

                    @blk.sync
                    def _(s, t2=t2):
                        _wait(s, v_c, v3_done[t2])
                        _inc(s.dma_start(out=xmid[:, t2], in_=xa[:]), st_x, 16)
                    st_xm[t2] = st_x.n

                    @SEC(blk, "vector")
                    def _(v, a2=a2):
                        v.tensor_copy(sqb[:], xa[:])
                        return v.tensor_tensor(a2, sqb[:], sqb[:], ALU.mult)
                    v_sq3 = v_c.n

                if tt < NT:
                    for m in range(2, 14):
                        @SEC(blk, "tensor")
                        def _(t, tt=tt, m=m):
                            return _wo(t, tt, m)
                        twm[tt * NCH + m] = t_c.n

                if t2 >= 0:
                    @SEC(blk, "tensor")
                    def _(t, v_sq3=v_sq3, a2=a2):
                        _wait(t, v_c, v_sq3)
                        for c in range(NCH):
                            t.matmul(ps[:, 2048:2048 + T], ones_bf[:, 0:128],
                                     sqb[:, c, :], start=(c == 0),
                                     stop=(c == NCH - 1),
                                     skip_group_check=True)
                        i = None
                        for c in range(NCH):
                            i = t.matmul(ps[:, 2560:2560 + T],
                                         ones_bf[:, 0:128], a2[:, c, :],
                                         start=(c == 0), stop=(c == NCH - 1),
                                         skip_group_check=True)
                        return i
                    t_sums3 = t_c.n

                if tt < NT:
                    for m in (14, 15):
                        @SEC(blk, "tensor")
                        def _(t, tt=tt, m=m):
                            return _wo(t, tt, m)
                        twm[tt * NCH + m] = t_c.n

                if t2 >= 0:
                    @SEC(blk, "vector")
                    def _(v, t_sums3=t_sums3):
                        return _stats(v, t_sums3)
                    v_rv3 = v_c.n

                if tt < NT:
                    @SEC(blk, "scalar")
                    def _(sc, tt=tt):
                        m2 = None
                        for m in range(2, NCH):
                            _wait(sc, t_c, twm[tt * NCH + m])
                            m2 = sc.copy(aoo[:, tt % 2, m, :],
                                         ps[:, (m % 2) * 512:
                                            (m % 2) * 512 + 512])
                            got = _inc(m2, c_c, 1)
                            assert got == ev[tt * NCH + m], (tt, m, got)
                        return None

                if t2 >= 0:
                    @SEC(blk, "scalar")
                    def _(sc, v_rv3=v_rv3):
                        _wait(sc, v_c, v_rv3)
                        return sc.activation(lnv[:, 0:T], lnv[:, 0:T], AF.Sqrt)
                    c_sqrt3 = c_c.n

                    @SEC(blk, "vector")
                    def _(v, c_sqrt3=c_sqrt3):
                        return _ab(v, c_sqrt3)

                    @SEC(blk, "vector")
                    def _(v, t2=t2, a2=a2):
                        v.tensor_tensor(a2, sqb[:], bc_ap(ab_bf[:, 0:T], NCH),
                                        ALU.mult)
                        return v.tensor_tensor(sqb[:], a2,
                                               bc_ap(ab_bf[:, 512:1024], NCH),
                                               ALU.add)
                    vlx_done[t2] = v_c.n

                    @blk.sync
                    def _(s, t2=t2):
                        _wait(s, v_c, vlx_done[t2])
                        _inc(s.dma_start(out=lx3[:, t2], in_=sqb[:]), st_lx, 16)
                    lx_st[t2] = st_lx.n

    G = 2
    NG = NIC // G
    with (
        nc.sbuf_tensor("c_xa", [128, NCH, T], F32) as xa2,
        nc.sbuf_tensor("c_ln3", [128, NCH, T], BF16) as ln3x,
        nc.sbuf_tensor("c_fuse", [128, NIC, T], BF16) as fuse,
        nc.sbuf_tensor("c_wgu", [128, 2, 2, G * D], BF16) as wgu_bf,
        nc.sbuf_tensor("c_wd", [128, 2, 4, D], BF16) as wd_bf,
        nc.sbuf_tensor("c_gb", [128, 4, T], BF16) as gb,
        nc.sbuf_tensor("c_ffo", [128, NCH, T], BF16) as ffo,
        nc.psum_tensor("c_ps", [128, 4096], F32) as ps,
    ):
        with nc.Block() as blk:
            block_guard(blk)
            sil_done = [None] * (NT * NG)
            upz_done = [None] * (NT * NG)
            dn_done = {}
            de_done = {}
            v3_done = [None] * NT
            st_out = [None] * NT
            g_wgu = [None] * (NT * NG)
            g_wd = {}
            gu_last = [None] * NT
            ln4_ab = [None] * NT

            for tt in range(NT):
                tc = slice(tt * T, (tt + 1) * T)

                @blk.sync
                def _(s, tt=tt, tc=tc):
                    _wait(s, st_x, st_xm[NT - 1] if tt == 0 else st_xm[tt])
                    if tt >= 1:
                        _wait(s, st_o, st_out[tt - 1])
                    _inc(s.dma_start(out=xa2[:], in_=xmid[:, tt]), s_xc, 16)
                xm_ld = s_xc.n

                if tt == 0:
                    @blk.sync
                    def _(s):
                        _wait(s, st_lx, lx_st[NT - 1])
                        _wait(s, st_x, st_xm[NT - 1])
                        _inc(s.dma_start(out=ln3x[:], in_=lx3[:, 0]), s_lx, 16)
                lx_ld = s_lx.n

                sil_jobs = []
                sil_base = c_c.n
                for grp in range(NG):
                    gi = tt * NG + grp
                    ic0 = grp * G
                    p2 = grp % 2

                    @blk.gpsimd
                    def _(g, tt=tt, grp=grp, gi=gi, ic0=ic0, p2=p2):
                        if gi == 0:
                            _wait(g, st_x, st_xm[NT - 1])
                            _wait(g, st_lx, lx_st[NT - 1])
                        if gi >= 2:
                            _wait(g, c_c, sil_done[gi - 2])
                            _wait(g, v_c, upz_done[gi - 2])
                        _inc(g.dma_start(out=wgu_bf[:, p2], in_=wgu_r[:, grp]),
                             g_g[p2], 16)
                    g_wgu[gi] = g_g[p2].n

                    @SEC(blk, "tensor")
                    def _(t, tt=tt, grp=grp, gi=gi, ic0=ic0, p2=p2):
                        _wait(t, g_g[p2], g_wgu[gi])
                        if grp <= 1:
                            _wait(t, s_lx, lx_ld)
                            if tt >= 1 and grp == 0:
                                _wait(t, v_b, de_done[(tt - 1, 1)])
                            if tt >= 1 and grp == 1:
                                _wait(t, v_c, ln4_ab[tt - 1])
                        if gi >= 2:
                            _wait(t, c_c, sil_done[gi - 2])
                            _wait(t, v_c, upz_done[gi - 2])
                        base = p2 * 2048
                        i = None
                        for j in range(G):
                            for c in range(NCH):
                                t.matmul(ps[:, base + j * 1024:base + j * 1024 + 512],
                                         wgu_bf[:, p2, 0, j * D + c * 128:
                                                j * D + (c + 1) * 128],
                                         ln3x[:, c, :],
                                         start=(c == 0), stop=(c == NCH - 1),
                                         skip_group_check=True)
                            for c in range(NCH):
                                i = t.matmul(ps[:, base + j * 1024 + 512:
                                                base + j * 1024 + 1024],
                                             wgu_bf[:, p2, 1, j * D + c * 128:
                                                   j * D + (c + 1) * 128],
                                             ln3x[:, c, :],
                                             start=(c == 0), stop=(c == NCH - 1),
                                             skip_group_check=True)
                        return i
                    t_gu = t_c.n
                    gu_last[tt] = t_gu

                    sil_jobs.append((grp, gi, ic0, p2, t_gu))
                    sil_done[gi] = sil_base + grp + 1

                    @SEC(blk, "vector")
                    def _(v, grp=grp, gi=gi, ic0=ic0, p2=p2, t_gu=t_gu):
                        _wait(v, t_c, t_gu)
                        _wait(v, c_c, sil_done[gi])
                        base = p2 * 2048
                        i = None
                        for j in range(G):
                            ic = ic0 + j
                            v.tensor_scalar_mul(fuse[:, ic, :],
                                                ps[:, base + j * 1024 + 512:
                                                   base + j * 1024 + 1024],
                                                su_sb[:, ic:ic + 1])
                            i = v.tensor_tensor(fuse[:, ic, :], gb[:, ic % 4],
                                                fuse[:, ic, :], ALU.mult)
                        return i
                    upz_done[gi] = v_c.n

                @SEC(blk, "scalar")
                def _(sc, sil_jobs=sil_jobs, tt=tt):
                    m = None
                    for grp, gi, ic0, p2, t_gu in sil_jobs:
                        _wait(sc, t_c, t_gu)
                        if gi == 0:
                            _wait(sc, s_sc, scales_ld)
                        base = p2 * 2048
                        for j in range(G):
                            ic = ic0 + j
                            m = sc.activation(gb[:, ic % 4],
                                              ps[:, base + j * 1024:
                                                 base + j * 1024 + 512],
                                              AF.Silu, scale=sg_sb[:, ic:ic + 1])
                        got = _inc(m, c_c, 1)
                        assert got == sil_done[tt * NG + grp], (tt, grp, got)
                    return None

                if tt + 1 < NT:
                    @SEC(blk, "scalar")
                    def _(sc, tt=tt):
                        _wait(sc, t_c, gu_last[tt])
                        _inc(sc.dma_start(out=ln3x[:], in_=lx3[:, tt + 1]),
                             s_lx, 16)
                        return None

                NGD = NIC // 4
                for half in range(2):
                    for gd in range(NGD):
                        ic0 = gd * 4
                        p2 = gd % 2

                        @blk.gpsimd
                        def _(g, tt=tt, half=half, gd=gd, ic0=ic0, p2=p2):
                            prev = None
                            if gd >= 2:
                                prev = (tt, half, gd - 2)
                            elif half == 1:
                                prev = (tt, 0, NGD - 2 + gd)
                            elif tt >= 1:
                                prev = (tt - 1, 1, NGD - 2 + gd)
                            if prev is not None:
                                _wait(g, t_b, dn_done[prev])
                            _inc(g.dma_start(
                                out=wd_bf[:, p2],
                                in_=wd_n[ic0 * 128:(ic0 + 4) * 128, :]
                                .rearrange("(a p) d -> p a d", p=128)),
                                 g_d[p2], 16)
                        g_wd[(tt, half, gd)] = g_d[p2].n

                        @SEC(blk, "tensor")
                        def _(t, tt=tt, half=half, gd=gd, ic0=ic0, p2=p2):
                            _wait(t, g_d[p2], g_wd[(tt, half, gd)])
                            if gd == 0:
                                _wait(t, v_c, upz_done[tt * NG + NG - 1])
                                if half == 0:
                                    _wait(t, c_c, sil_done[tt * NG + NG - 1])
                                else:
                                    _wait(t, v_b, de_done[(tt, 0)])
                            i = None
                            for j in range(4):
                                ic = ic0 + j
                                for m in range(8):
                                    mm = half * 8 + m
                                    i = t.matmul(ps[:, m * 512:(m + 1) * 512],
                                                 wd_bf[:, p2, j,
                                                       mm * 128:(mm + 1) * 128],
                                                 fuse[:, ic, :],
                                                 start=(ic == 0),
                                                 stop=(ic == NIC - 1),
                                                 skip_group_check=True)
                            _inc(i, t_b, 1)
                        dn_done[(tt, half, gd)] = t_b.n

                    @SEC(blk, "vector")
                    def _(v, tt=tt, half=half):
                        _wait(v, t_b, dn_done[(tt, half, NGD - 1)])
                        i = None
                        for m in range(8):
                            mm = half * 8 + m
                            i = v.tensor_scalar_mul(ffo[:, mm, :],
                                                    ps[:, m * 512:(m + 1) * 512],
                                                    sd_sb[:, mm:mm + 1])
                        _inc(i, v_b, 1)
                    de_done[(tt, half)] = v_b.n

                fv = fuse[:, 0:NCH, :]
                ln4_ab[tt] = emit_ln(blk, ps, ffo[:], fv, T)

                @SEC(blk, "vector")
                def _(v, tt=tt, fv=fv, xm_ld=xm_ld):
                    _wait(v, s_xc, xm_ld)
                    v.tensor_tensor(fv, ffo[:], bc_ap(ab_bf[:, 0:T], NCH), ALU.mult)
                    v.tensor_tensor(ffo[:], fv, bc_ap(ab_bf[:, 512:1024], NCH),
                                    ALU.add)
                    return v.tensor_tensor(xa2[:], xa2[:], ffo[:], ALU.add)
                v3_done[tt] = v_c.n

                @blk.sync
                def _(s, tt=tt, tc=tc):
                    _wait(s, v_c, v3_done[tt])
                    _inc(s.dma_start(out=outR[:, tt], in_=xa2[:]), st_o, 16)
                st_out[tt] = st_o.n

            @blk.sync
            def _(s):
                _wait(s, st_o, st_out[NT - 1])

    for sem, v in waits:
        assert v <= sem.n, f"wait {v} > final {sem.n} on {sem.h}"
    return nc


_CACHE = {}


def _prep_weights(wq, wk, wv, wo, wg, wu, wd, sg, su, sd):
    f32 = np.float32
    wq_r = np.ascontiguousarray(
        wq.T.reshape(NCH, 128, NH, 128).transpose(1, 2, 0, 3).reshape(128, -1),
        dtype=f32)
    wk_r = np.ascontiguousarray(
        wk.T.reshape(NCH, 128, 1024).transpose(1, 0, 2).reshape(128, -1), dtype=f32)
    wv_r = np.ascontiguousarray(
        wv.T.reshape(NCH, 128, 1024).transpose(1, 0, 2).reshape(128, -1), dtype=f32)
    wo_r = np.ascontiguousarray(
        wo.T.reshape(NCH, 128, D).transpose(1, 0, 2).reshape(128, -1), dtype=f32)
    wg_r = wg.reshape(NCH, 128, NIC, 128).transpose(1, 2, 0, 3).reshape(128, NIC // 2, 2 * D)
    wu_r = wu.reshape(NCH, 128, NIC, 128).transpose(1, 2, 0, 3).reshape(128, NIC // 2, 2 * D)
    wgu_r = np.ascontiguousarray(np.stack([wg_r, wu_r], axis=2))
    wd_n = np.ascontiguousarray(wd)
    sg_r = np.ascontiguousarray(sg.reshape(NIC, 128).T, dtype=f32)
    su_r = np.ascontiguousarray(su.reshape(NIC, 128).T, dtype=f32)
    sd_r = np.ascontiguousarray(sd.reshape(NCH, 128).T, dtype=f32)
    return dict(wq_r=wq_r, wk_r=wk_r, wv_r=wv_r, wo_r=wo_r, wgu_r=wgu_r,
                wd_n=wd_n, sg_r=sg_r, su_r=su_r, sd_r=sd_r)


def kernel(L0_attn_hidden_states, L0_attn_wq, L0_attn_wk, L0_attn_wv,
           L0_attn_wo, L0_ffn_w_gate, L0_ffn_w_up, L0_ffn_w_down,
           L0_ffn_s_gate, L0_ffn_s_up, L0_ffn_s_down):
    x = np.asarray(L0_attn_hidden_states, dtype=np.float32)
    B = x.shape[0]
    shared = _prep_weights(
        np.asarray(L0_attn_wq, dtype=np.float32),
        np.asarray(L0_attn_wk, dtype=np.float32),
        np.asarray(L0_attn_wv, dtype=np.float32),
        np.asarray(L0_attn_wo, dtype=np.float32),
        np.asarray(L0_ffn_w_gate, dtype=np.int8),
        np.asarray(L0_ffn_w_up, dtype=np.int8),
        np.asarray(L0_ffn_w_down, dtype=np.int8),
        np.asarray(L0_ffn_s_gate, dtype=np.float32),
        np.asarray(L0_ffn_s_up, dtype=np.float32),
        np.asarray(L0_ffn_s_down, dtype=np.float32))

    if "nc" not in _CACHE:
        _CACHE["nc"] = build_program()
    nc = _CACHE["nc"]

    in_maps = []
    for b in range(B):
        m = dict(shared)
        m["xR"] = np.ascontiguousarray(
            x[b].T.reshape(NCH, 128, NT, T).transpose(1, 2, 0, 3))
        in_maps.append(m)

    res = run_bass_kernel_spmd(nc, in_maps, core_ids=list(range(B)))
    out = np.stack([np.asarray(res.results[b]["outR"]).transpose(1, 3, 2, 0)
                    .reshape(S, D) for b in range(B)], axis=0)
    return np.ascontiguousarray(out, dtype=np.float32)
